# version 10
# baseline (speedup 1.0000x reference)
import contextlib
import math
import numpy as np
import concourse.bass as bass
import concourse.mybir as mybir
from concourse.bass_utils import run_bass_kernel_spmd

F32 = mybir.dt.float32
F32R = mybir.dt.float32r
BF16 = mybir.dt.bfloat16
I32 = mybir.dt.int32
AF = mybir.ActivationFunctionType
ALU = mybir.AluOpType
AX = mybir.AxisListType

D = 2048
NT = 2048
H = 16
E = 128
CH = 128
NCH = NT // CH
FF = 1408
NII = FF // 128
NE = 8
NPATH = 9
KK = D // 128
TT = NT // 512
PI = math.pi
C1 = 6.28125
C2 = 2 * math.pi - C1
INV2PI = 1.0 / (2 * math.pi)
SLOPES = [2.0 ** (-0.5 * (j + 1)) for j in range(H)]

ENGINES = ("tensor", "vector", "scalar", "gpsimd", "sync")


class Sched:

    def __init__(self):
        self.ops = {e: [] for e in ENGINES}
        self.planned = {}
        self.pending = {e: [] for e in ENGINES}

    def fence(self, extra=()):
        waits = [(s, v) for s, v in self.planned.items()] + list(extra)
        for e in ENGINES:
            self.pending[e] = list(waits)

    def add(self, eng, fn, waits=(), inc=None, sync=False):
        incval = None
        if inc is not None:
            v = self.planned.get(inc[0], 0) + inc[1]
            self.planned[inc[0]] = v
            incval = v
        allw = list(self.pending[eng]) + [(s, v) for (s, v) in waits if v is not None and v > 0]
        self.pending[eng] = []
        self.ops[eng].append((tuple(allw), fn, inc, sync))
        return incval

    def emit_all(self, block, sems):
        for engname in ENGINES:
            oplist = self.ops[engname]
            def make(engname=engname, oplist=oplist):
                def body(eng):
                    lw = {}
                    for waits, fn, inc, opsync in oplist:
                        for s, v in waits:
                            if v > lw.get(s, 0):
                                eng.wait_ge(sems[s], v)
                                lw[s] = v
                        ins = fn()
                        if inc is not None:
                            if isinstance(ins, (list, tuple)):
                                ins = ins[-1]
                            if engname == "scalar" and inc[1] < 16:
                                eng.drain().then_inc(sems[inc[0]], inc[1])
                            elif engname == "gpsimd" and inc[1] < 16:
                                eng.engine_nop().then_inc(sems[inc[0]], inc[1])
                            elif engname == "vector" and opsync and inc[1] < 16:
                                eng.drain().then_inc(sems[inc[0]], inc[1])
                            else:
                                ins.then_inc(sems[inc[0]], inc[1])
                return body
            getattr(block, engname)(make())


def build(phases=("p0", "p1", "p2", "p3", "p4", "p5", "p6"), dbg=()):
    nc = bass.Bass("TRN2", target_bir_lowering=False)
    ap = lambda t: t.ap()

    xo = ap(nc.dram_tensor("xo", [NT, D], F32, kind="ExternalInput"))
    xh = ap(nc.dram_tensor("xh", [NT, D], F32, kind="ExternalInput"))
    wqkv = ap(nc.dram_tensor("wqkv", [48, 128, KK * 128], F32R, kind="ExternalInput"))
    wproj = ap(nc.dram_tensor("wproj", [16, 128, KK * 128], F32R, kind="ExternalInput"))
    wgu = ap(nc.dram_tensor("wgu", [NPATH, NII, 2, 128, 8 * 2 * 128], F32, kind="ExternalInput"))
    wdn = ap(nc.dram_tensor("wdn", [NPATH, 16, 128, NII * 128], F32, kind="ExternalInput"))
    wrt = ap(nc.dram_tensor("wrt", [128, KK * 8], F32, kind="ExternalInput"))
    theta = ap(nc.dram_tensor("theta", [H, E], F32, kind="ExternalInput"))
    t0o = ap(nc.dram_tensor("t0o", [128, 1], F32, kind="ExternalInput"))
    t0h = ap(nc.dram_tensor("t0h", [128, 1], F32, kind="ExternalInput"))
    muse = ap(nc.dram_tensor("muse", [128, 1], F32, kind="ExternalInput"))
    out = ap(nc.dram_tensor("out", [NT, D], F32, kind="ExternalOutput"))

    xT_scr = ap(nc.dram_tensor("xT_scr", [D, NT], F32, kind="Internal"))
    qkvT = ap(nc.dram_tensor("qkvT_scr", [3 * D, NT], F32, kind="Internal"))
    kvT = ap(nc.dram_tensor("kvT_scr", [2 * D, NT], F32, kind="Internal"))
    oT_scr = ap(nc.dram_tensor("oT_scr", [D, NT], F32, kind="Internal"))
    tmp_scr = ap(nc.dram_tensor("tmp_scr", [D, NT], F32, kind="Internal"))

    dbg_outs = {}
    dbg_specs = {"qkvT": ([3 * D, NT], F32), "kvT": ([2 * D, NT], F32),
                 "oT": ([D, NT], F32), "tmp": ([D, NT], F32),
                 "x1b": ([128, KK * NT], BF16), "acc": ([128, KK * NT], BF16),
                 "comb": ([128, NCH * 8], F32), "srow": ([128, NCH * 8], F32), "p4log": ([128, 8 * 32], F32)}
    for name in dbg:
        if name not in dbg_specs:
            continue
        shp, dt = dbg_specs[name]
        dbg_outs[name] = ap(nc.dram_tensor("dbg_" + name, shp, dt, kind="ExternalOutput"))

    S = Sched()
    es = contextlib.ExitStack()
    with es:
        sems = {s: es.enter_context(nc.semaphore(s))
                for s in ["ld", "st", "mm", "ve", "ac", "gp", "gd"]}
        sb = lambda n, shp, dt: es.enter_context(nc.sbuf_tensor(n, shp, dt))

        consts = sb("consts", [128, 12], F32)
        idf = sb("idf", [128, 128], F32)
        idb = sb("idb", [128, 128], BF16)
        ones1 = sb("ones1", [1, 128], F32R)
        iobuf = sb("iobuf", [128, 128], F32)
        thetaT = sb("thetaT", [128, H], F32)
        musec = sb("musec", [128, 1], F32)
        t0oc = sb("t0oc", [128, 1], F32)
        t0hc = sb("t0hc", [128, 1], F32)

        pre_scope = contextlib.ExitStack()
        theta_st = pre_scope.enter_context(nc.sbuf_tensor("theta_st", [16, 128], F32))
        pre_ps = nc.psum_tensor("pre_ps", [128, 512], F32)
        with pre_ps as pre_ps:
            m_th = S.add("sync", lambda: nc.sync.dma_start(out=theta_st[:], in_=theta), inc=("ld", 16))
            S.add("sync", lambda: nc.sync.dma_start(out=t0oc[:], in_=t0o), inc=("ld", 16))
            S.add("sync", lambda: nc.sync.dma_start(out=t0hc[:], in_=t0h), inc=("ld", 16))
            m_aux = S.add("sync", lambda: nc.sync.dma_start(out=musec[:], in_=muse), inc=("ld", 16))
            m_io = S.add("gpsimd", lambda: nc.gpsimd.iota(iobuf[:], pattern=[[1, 128]], base=0,
                                                          channel_multiplier=-1,
                                                          allow_small_or_imprecise_dtypes=True), inc=("gp", 1))
            m_idf = S.add("vector", lambda: nc.vector.tensor_single_scalar(idf[:], iobuf[:], 0.0, ALU.is_equal),
                          waits=[("gp", m_io)], inc=("ve", 1))
            S.add("vector", lambda: nc.vector.tensor_copy(idb[:], idf[:]), inc=("ve", 1))
            S.add("vector", lambda: nc.vector.memset(ones1[:].bitcast(F32), 1.0), inc=("ve", 1))
            S.add("vector", lambda: nc.vector.memset(consts[:, 0:1], -1.0), inc=("ve", 1))
            S.add("vector", lambda: nc.vector.memset(consts[:, 1:2], 1e-5), inc=("ve", 1))
            S.add("vector", lambda: nc.vector.memset(consts[:, 2:3], 1.0), inc=("ve", 1))
            S.add("vector", lambda: nc.vector.memset(consts[:, 3:4], 0.0), inc=("ve", 1))
            S.add("vector", lambda: nc.vector.memset(consts[:, 4:5], -0.5), inc=("ve", 1))
            S.add("vector", lambda: nc.vector.memset(consts[:, 5:6], INV2PI), inc=("ve", 1))
            S.add("vector", lambda: nc.vector.memset(consts[:, 6:7], -PI), inc=("ve", 1))
            S.add("vector", lambda: nc.vector.memset(consts[:, 7:8], 2 * PI), inc=("ve", 1))
            S.add("vector", lambda: nc.vector.memset(consts[:, 8:9], PI / 2), inc=("ve", 1))
            S.add("vector", lambda: nc.vector.memset(consts[:, 9:10], PI), inc=("ve", 1))
            m_cst = S.add("vector", lambda: nc.vector.memset(consts[:, 11:12], 1.0 / D), inc=("ve", 1))
            for fn in (AF.Exp, AF.Sin, AF.Square, AF.Ln, AF.Sigmoid):
                S.add("scalar", lambda fn=fn: nc.scalar.activation(consts[:, 10:11], consts[:, 2:3], fn),
                      waits=[("ve", m_cst)], inc=("ac", 1))
            m_warm = S.add("scalar", lambda: nc.scalar.activation(consts[:, 10:11], consts[:, 2:3], AF.Exp),
                           inc=("ac", 1))
            ms_warm = m_warm
            m_tht = S.add("tensor", lambda: nc.tensor.transpose(pre_ps[:, 0:16], theta_st[:], idf[0:16, 0:16]),
                          waits=[("ld", m_th), ("ve", m_idf)], inc=("mm", 1))
            m_thT = S.add("vector", lambda: nc.vector.tensor_copy(thetaT[:], pre_ps[:, 0:16]),
                          waits=[("mm", m_tht)], inc=("ve", 1))

        pre_scope.close()
        ms = {"pre_ve": m_thT, "pre_gp": m_io, "aux_ld": m_aux}

        last_p1_mm = None
        if "p0" in phases:
            p01 = contextlib.ExitStack()
            with p01:
                xT = p01.enter_context(nc.sbuf_tensor("xT", [128, KK * NT], F32R))
                xld = [p01.enter_context(nc.sbuf_tensor(f"xld{i}", [128, D], F32)) for i in range(2)]
                wqb = [p01.enter_context(nc.sbuf_tensor(f"wqb{i}", [128, KK * 128], F32R)) for i in range(2)]
                qrow = [p01.enter_context(nc.sbuf_tensor(f"qrow{i}", [128, NT], F32)) for i in range(2)]
                ps = [p01.enter_context(nc.psum_tensor(f"p1ps{i}", [128, 512], F32)) for i in range(8)]
                bank_free = [ms["pre_ve"]] * 8

                def build_xT(xsrc, first_waits):
                    ld_ms = [None] * NCH
                    tg_done = [None] * NCH
                    for tg in range(NCH):
                        buf = xld[tg % 2]
                        w = (list(first_waits) + [("ve", ms["pre_ve"])]) if tg == 0 else []
                        if tg >= 2:
                            w.append(("ve", tg_done[tg - 2]))
                        ld_ms[tg] = S.add("sync", lambda buf=buf, tg=tg, xsrc=xsrc:
                                          nc.sync.dma_start(out=buf[:], in_=xsrc[tg * 128:(tg + 1) * 128, :]),
                                          waits=w, inc=("ld", 16))
                        for g4 in range(4):
                            b = (tg % 2) * 4 + g4
                            bank = ps[b]
                            tw = [("ld", ld_ms[tg]), ("ve", bank_free[b])]
                            mlast = None
                            for j in range(4):
                                kkc = g4 * 4 + j
                                mlast = S.add("tensor", lambda bank=bank, j=j, kkc=kkc, buf=buf:
                                              nc.tensor.transpose(bank[:, j * 128:(j + 1) * 128],
                                                                  buf[:, kkc * 128:(kkc + 1) * 128], idf[:]),
                                              waits=(tw if j == 0 else []), inc=("mm", 1))

                            def ev(bank=bank, g4=g4, tg=tg):
                                ins = None
                                for j in range(4):
                                    kkc = g4 * 4 + j
                                    ins = nc.vector.tensor_copy(
                                        xT[:, kkc * NT + tg * 128: kkc * NT + (tg + 1) * 128],
                                        bank[:, j * 128:(j + 1) * 128])
                                return ins
                            bank_free[b] = S.add("vector", ev, waits=[("mm", mlast)], inc=("ve", 1))
                        tg_done[tg] = bank_free[(tg % 2) * 4 + 3]
                    return tg_done[NCH - 1], max(filter(None, [bank_free[i] for i in range(8)]))

                def spill_xT(last_ve):
                    mlast = None
                    for kkc in range(KK):
                        mlast = S.add("scalar", lambda kkc=kkc:
                                      nc.scalar.dma_start(out=xT_scr[kkc * 128:(kkc + 1) * 128, :],
                                                          in_=xT[:, kkc * NT:(kkc + 1) * NT].bitcast(F32)),
                                      waits=([("ve", last_ve)] if kkc == 0 else []), inc=("st", 16))
                    return mlast

                def qkv_pass(ob_list, dst, xT_ready_ve, start_idx_rows):
                    n = len(ob_list)
                    wld = [None] * n
                    blk_mm = [None] * n
                    evac_ve = [None] * n
                    spill_st = [None] * n
                    for i, ob in enumerate(ob_list):
                        wbuf = wqb[i % 2]
                        w = []
                        if i >= 2:
                            w.append(("mm", blk_mm[i - 2]))
                        wld[i] = S.add("sync", lambda wbuf=wbuf, ob=ob:
                                       nc.sync.dma_start(out=wbuf[:], in_=wqkv[ob]),
                                       waits=w, inc=("ld", 16))
                        mlast = None
                        for kk in range(KK):
                            for tt in range(TT):
                                b = (i % 2) * 4 + tt
                                tw = []
                                if kk == 0:
                                    tw = [("ld", wld[i]), ("ve", bank_free[b])]
                                    if tt == 0:
                                        tw.append(("ve", xT_ready_ve))
                                mlast = S.add("tensor", lambda b=b, kk=kk, tt=tt, wbuf=wbuf:
                                              nc.tensor.matmul(ps[b][:, 0:512],
                                                               wbuf[:, kk * 128:(kk + 1) * 128],
                                                               xT[:, kk * NT + tt * 512: kk * NT + (tt + 1) * 512],
                                                               start=(kk == 0), stop=(kk == KK - 1)),
                                              waits=tw, inc=("mm", 1) if (kk == KK - 1 and tt == TT - 1) else None)
                        blk_mm[i] = S.planned["mm"]

                        qb = qrow[i % 2]

                        def ev(i=i, qb=qb):
                            ins = None
                            for tt in range(TT):
                                b = (i % 2) * 4 + tt
                                ins = nc.vector.tensor_copy(qb[:, tt * 512:(tt + 1) * 512], ps[b][:, 0:512])
                            return ins
                        wv = [("mm", blk_mm[i])]
                        if i >= 2:
                            wv.append(("st", spill_st[i - 2]))
                        evac_ve[i] = S.add("vector", ev, waits=wv, inc=("ve", 1))
                        for b in range((i % 2) * 4, (i % 2) * 4 + 4):
                            bank_free[b] = evac_ve[i]
                        r0 = (start_idx_rows + i) * 128
                        spill_st[i] = S.add("scalar", lambda qb=qb, r0=r0, dst=dst:
                                            nc.scalar.dma_start(out=dst[r0:r0 + 128, :], in_=qb[:]),
                                            waits=[("ve", evac_ve[i])], inc=("st", 16))
                    return blk_mm[n - 1], spill_st[n - 1], max(spill_st)

                own_last_ve, _ = build_xT(xo, [])
                m_spill_xT = spill_xT(own_last_ve)
                p1a_mm, _, p1a_st = qkv_pass(list(range(48)), qkvT, own_last_ve, 0)
                oth_last_ve, _ = build_xT(xh, [("mm", p1a_mm)])
                p1b_mm, _, p1b_st = qkv_pass(list(range(16, 48)), kvT, oth_last_ve, 0)
                last_p1_mm = p1b_mm
                ms["p1_st"] = max(p1a_st, p1b_st, m_spill_xT)
                ms["p1_mm"] = p1b_mm


        if "p2" in phases:
            p2 = contextlib.ExitStack()
            with p2:
                sb2 = lambda n, shp, dt: p2.enter_context(nc.sbuf_tensor(n, shp, dt))
                qh = sb2("qh", [128, NT], F32)
                kh = sb2("kh", [128, NT], F32)
                vh = sb2("vh", [128, NT], F32)
                kho = sb2("kho", [128, NT], F32)
                vho = sb2("vho", [128, NT], F32)
                Ctab = sb2("Ctab", [128, 2 * NT], F32)
                Stab = sb2("Stab", [128, 2 * NT], F32)
                thb = sb2("thb", [128, 2 * NT], F32)
                t1b = sb2("t1b", [128, 2 * NT], F32)
                t2b = sb2("t2b", [128, 2 * NT], F32)
                QC = sb2("QC", [128, NT], BF16)
                QS = sb2("QS", [128, NT], BF16)
                QCD = sb2("QCD", [128, NT], BF16)
                QSD = sb2("QSD", [128, NT], BF16)
                KCt = sb2("KCt", [128, NT], BF16)
                KSt = sb2("KSt", [128, NT], BF16)
                qgc = sb2("qgc", [128, NT], BF16)
                qgs = sb2("qgs", [128, NT], BF16)
                vtok = sb2("vtok", [128, NT], BF16)
                vdec = sb2("vdec", [128, NT], BF16)
                vtoko = sb2("vtoko", [128, NT], BF16)
                vdeco = sb2("vdeco", [128, NT], BF16)
                o_h = sb2("o_h", [128, NT], F32)
                POSo = sb2("POSo", [128, 2 * NT], F32)
                decayT = sb2("decayT", [128, H * 128], BF16)
                kdecc = sb2("kdecc", [128, H], F32)
                dfc = sb2("dfc", [128, 128], F32)
                c127 = sb2("c127", [128, 1], F32)
                dtcol = sb2("dtcol", [128, 1], F32)
                thd = sb2("thd", [128, 1], F32)
                qdrow = sb2("qdrow", [1, 512], F32R)
                QD512 = sb2("QD512", [128, 512], F32)
                stC = sb2("stC", [128, 128], F32)
                stS = sb2("stS", [128, 128], F32)
                stCb = sb2("stCb", [128, 128], BF16)
                stSb = sb2("stSb", [128, 128], BF16)
                Scb = sb2("Scb", [128, 128], BF16)
                Ssb = sb2("Ssb", [128, 128], BF16)
                kct = sb2("kct", [128, 128], BF16)
                kst = sb2("kst", [128, 128], BF16)
                scTb = sb2("scTb", [128, 128], BF16)

                ps_sc = p2.enter_context(nc.psum_tensor("ps_sc", [128, 512], F32))
                ps_o = p2.enter_context(nc.psum_tensor("ps_o", [128, 512], F32))
                ps_s1 = p2.enter_context(nc.psum_tensor("ps_s1", [128, 512], F32))
                ps_s2 = p2.enter_context(nc.psum_tensor("ps_s2", [128, 512], F32))
                pbf = p2.enter_context(nc.psum_tensor("pbf", [128, 1024], BF16))
                ps_c0 = p2.enter_context(nc.psum_tensor("ps_c0", [128, 512], F32))
                ps_c1 = p2.enter_context(nc.psum_tensor("ps_c1", [128, 512], F32))
                ps_cc = [ps_c0, ps_c1]

                S.fence()

                m_pos0 = S.add("gpsimd", lambda: nc.gpsimd.iota(POSo[:, 0:NT], pattern=[[1, NT]], base=0,
                                                                channel_multiplier=0,
                                                                allow_small_or_imprecise_dtypes=True), inc=("gp", 1))
                m_posb = S.add("gpsimd", lambda: nc.gpsimd.tensor_tensor(
                    POSo[:, NT:2 * NT], POSo[:, 0:NT], t0hc[:, 0:1].to_broadcast([128, NT]), ALU.add),
                               waits=[("ld", ms["aux_ld"]), ("ve", ms["pre_ve"])], inc=("gp", 1))
                m_pos = S.add("gpsimd", lambda: nc.gpsimd.tensor_tensor(
                    POSo[:, 0:NT], POSo[:, 0:NT], t0oc[:, 0:1].to_broadcast([128, NT]), ALU.add),
                              inc=("gp", 1))
                m_dt = S.add("vector", lambda: nc.vector.tensor_sub(dtcol[:], t0hc[:], t0oc[:]),
                             waits=[("ld", ms["aux_ld"])], inc=("ve", 1))
                m_dfc = S.add("vector", lambda: nc.vector.tensor_scalar_max(dfc[:], iobuf[:], 0.0), inc=("ve", 1))
                m_c127 = S.add("gpsimd", lambda: nc.gpsimd.iota(c127[:], pattern=[[0, 1]], base=127,
                                                                channel_multiplier=-1,
                                                                allow_small_or_imprecise_dtypes=True), inc=("gp", 1))
                m_arg = None
                for h in range(H):
                    s = SLOPES[h]
                    m_arg = S.add("vector", lambda h=h, s=s:
                                  nc.vector.tensor_scalar_mul(decayT[:, h * 128:(h + 1) * 128], dfc[:], -s),
                                  waits=[("ve", m_dfc)], inc=("ve", 1))
                    m_arg = S.add("vector", lambda h=h, s=s:
                                  nc.vector.tensor_scalar_mul(kdecc[:, h:h + 1], c127[:], -s),
                                  waits=[("gp", m_c127)], inc=("ve", 1))
                m_exp1 = S.add("scalar", lambda: nc.scalar.activation(decayT[:], decayT[:], AF.Exp),
                               waits=[("ve", m_arg)], inc=("ac", 1))
                m_exp2 = S.add("scalar", lambda: nc.scalar.activation(kdecc[:], kdecc[:], AF.Exp),
                               inc=("ac", 1))
                m_mask = None
                for h in range(H):
                    m_mask = S.add("gpsimd", lambda h=h:
                                   nc.gpsimd.affine_select(decayT[:, h * 128:(h + 1) * 128],
                                                           decayT[:, h * 128:(h + 1) * 128],
                                                           pattern=[[1, 128]], base=0, channel_multiplier=-1,
                                                           compare_op=ALU.is_ge, fill=0.0),
                                   waits=[("ac", m_exp1)], inc=("gp", 1))
                ms["p2_pre_gp"] = m_mask
                ms["p2_pre_ac"] = m_exp2

                sin_state = {"last_ac": None}
                W2 = 2 * NT

                def gen_tables(h, wait_before):
                    thc = thetaT[:, h:h + 1]
                    w0 = list(wait_before) + [("gp", m_pos), ("ve", ms["pre_ve"])]
                    w0 += tabstate["prev"]
                    gth = S.add("gpsimd", lambda thc=thc: nc.gpsimd.tensor_tensor(
                        thb[:], POSo[:], thc.to_broadcast([128, W2]), ALU.mult),
                                waits=w0, inc=("gp", 1))
                    g1 = S.add("gpsimd", lambda: nc.gpsimd.tensor_tensor(
                        t1b[:], thb[:], consts[:, 5:6].to_broadcast([128, W2]), ALU.mult),
                               inc=("gp", 1))
                    v1 = S.add("vector", lambda: nc.vector.tensor_copy(t2b[:].bitcast(I32), t1b[:]),
                               waits=[("gp", g1)], inc=("ve", 1))
                    v2 = S.add("vector", lambda: nc.vector.tensor_copy(t1b[:], t2b[:].bitcast(I32)),
                               inc=("ve", 1))
                    v3 = S.add("vector", lambda: nc.vector.scalar_tensor_tensor(
                        t2b[:], t1b[:], -2 * PI, thb[:], ALU.mult, ALU.add), inc=("ve", 1))
                    v4 = S.add("vector", lambda: nc.vector.tensor_single_scalar(
                        t1b[:], t2b[:], PI, ALU.is_gt), inc=("ve", 1))
                    v5 = S.add("vector", lambda: nc.vector.scalar_tensor_tensor(
                        t2b[:], t1b[:], -2 * PI, t2b[:], ALU.mult, ALU.add), inc=("ve", 1))
                    a_sin = S.add("scalar", lambda: nc.scalar.activation(Stab[:], t2b[:], AF.Sin,
                                                                         scale=consts[:, 2:3]),
                                  waits=[("ve", v5)], inc=("ac", 1))
                    v6 = S.add("vector", lambda: nc.vector.tensor_scalar(
                        t1b[:], t2b[:], -1.0, PI / 2, ALU.mult, ALU.add), inc=("ve", 1))
                    v7 = S.add("vector", lambda: nc.vector.tensor_single_scalar(
                        thb[:], t1b[:], PI, ALU.is_gt), inc=("ve", 1))
                    v8 = S.add("vector", lambda: nc.vector.scalar_tensor_tensor(
                        t1b[:], thb[:], -2 * PI, t1b[:], ALU.mult, ALU.add), inc=("ve", 1))
                    a_cos = S.add("scalar", lambda: nc.scalar.activation(Ctab[:], t1b[:], AF.Sin,
                                                                         scale=consts[:, 2:3]),
                                  waits=[("ve", v8)], inc=("ac", 1))
                    return a_cos

                def geo_row(dst_row, n, s, wait_ve):
                    r = dst_row
                    m = S.add("vector", lambda r=r: nc.vector.memset(r[0:1, 0:1].bitcast(F32), math.exp(-s)),
                              waits=wait_ve, inc=("ve", 1), sync=True)
                    L = 1
                    while L < n:
                        c = min(L, n - L)
                        m = S.add("vector", lambda L=L, c=c, r=r: nc.vector.tensor_scalar_mul(
                            r[0:1, L:L + c], r[0:1, 0:c], math.exp(-s * L)), inc=("ve", 1),
                                  sync=True)
                        L += c
                    return m

                def bcast_rows(row_ap, dst, ncols, bank, first_waits):
                    last_ve = None
                    for t in range(0, ncols, 512):
                        w = min(512, ncols - t)
                        mmv = S.add("tensor", lambda t=t, w=w: nc.tensor.matmul(
                            bank[:, 0:w], ones1[:], row_ap[0:1, t:t + w],
                            start=True, stop=True),
                                    waits=(first_waits if t == 0 else [("ve", last_ve)]),
                                    inc=("mm", 1))
                        last_ve = S.add("vector", lambda t=t, w=w, dst=dst: nc.vector.tensor_copy(
                            dst[:, t:t + w], bank[:, 0:w]),
                                        waits=[("mm", mmv)], inc=("ve", 1))
                    return last_ve

                prev_head_mm = ms["p1_mm"]
                tabstate = {"prev": []}
                prev_corr_ve = None
                prev_spill_st = None
                pbf_free = [None] * 8
                ld_hd = {}

                def head_loads(h, waits):
                    r = {}
                    src_rows = [("qh", qh, qkvT, h * 128), ("kh", kh, qkvT, D + h * 128),
                                ("vh", vh, qkvT, 2 * D + h * 128),
                                ("kho", kho, kvT, h * 128), ("vho", vho, kvT, D + h * 128)]
                    for nm, buf, scr, r0 in src_rows:
                        r[nm] = S.add("sync", lambda buf=buf, scr=scr, r0=r0:
                                      nc.sync.dma_start(out=buf[:], in_=scr[r0:r0 + 128, :]),
                                      waits=waits, inc=("ld", 16))
                    return r

                ld_hd[0] = head_loads(0, [("st", ms["p1_st"])])

                for h in range(H):
                    s = SLOPES[h]
                    bd = math.exp(-s * CH)
                    dT = decayT[:, h * 128:(h + 1) * 128]

                    m_z = S.add("vector", lambda: nc.vector.memset(stC[:], 0.0),
                                waits=[("mm", prev_head_mm)], inc=("ve", 1))
                    m_z = S.add("vector", lambda: nc.vector.memset(stS[:], 0.0), inc=("ve", 1))
                    m_zb = S.add("vector", lambda: nc.vector.memset(stCb[:], 0.0), inc=("ve", 1))
                    m_zb = S.add("vector", lambda: nc.vector.memset(stSb[:], 0.0), inc=("ve", 1))

                    a_tab = gen_tables(h, [("mm", prev_head_mm), ("gp", ms["p2_pre_gp"])])
                    wprep = [("ac", a_tab), ("ld", ld_hd[h]["kho"]), ("mm", prev_head_mm)]
                    m_kco = S.add("vector", lambda: nc.vector.tensor_mul(KCt[:], kho[:], Ctab[:, NT:2 * NT]),
                                  waits=wprep, inc=("ve", 1))
                    m_kso = S.add("vector", lambda: nc.vector.tensor_mul(KSt[:], kho[:], Stab[:, NT:2 * NT]),
                                  inc=("ve", 1), sync=True)
                    vevacs = []
                    for g4 in range(4):
                        bank = ps_cc[g4 % 2]
                        wv = [("ld", ld_hd[h]["vho"]), ("mm", prev_head_mm)]
                        if prev_corr_ve is not None:
                            wv.append(("ve", prev_corr_ve))
                        if len(vevacs) >= 2:
                            wv.append(("ac", vevacs[-2]))
                        ml = None
                        for j in range(4):
                            c = g4 * 4 + j
                            ml = S.add("tensor", lambda bank=bank, j=j, c=c: nc.tensor.transpose(
                                bank[:, j * 128:(j + 1) * 128], vho[:, c * 128:(c + 1) * 128], idf[:]),
                                       waits=(wv if j == 0 else []), inc=("mm", 1))

                        def vev(bank=bank, g4=g4):
                            ins = None
                            for j in range(4):
                                c = g4 * 4 + j
                                ins = nc.scalar.copy(vtoko[:, c * 128:(c + 1) * 128],
                                                     bank[:, j * 128:(j + 1) * 128])
                            return ins
                        vevacs.append(S.add("scalar", vev, waits=[("mm", ml)], inc=("ac", 1)))
                    m_vd = S.add("vector", lambda h=h: nc.vector.tensor_scalar_mul(
                        vdeco[:], vtoko[:], kdecc[:, h:h + 1]),
                                 waits=[("ac", vevacs[-1]), ("ac", ms["p2_pre_ac"])], inc=("ve", 1), sync=True)

                    st_up = m_z
                    kT_ve = None
                    for c in range(NCH):
                        slot = (c % 4) * 2
                        wv = [("ve", m_kso)]
                        if pbf_free[slot] is not None:
                            wv.append(pbf_free[slot])
                        t1 = S.add("tensor", lambda slot=slot, c=c: nc.tensor.transpose(
                            pbf[:, slot * 128:(slot + 1) * 128], KCt[:, c * 128:(c + 1) * 128], idb[:]),
                                   waits=wv, inc=("mm", 1))
                        t2m = S.add("tensor", lambda slot=slot, c=c: nc.tensor.transpose(
                            pbf[:, (slot + 1) * 128:(slot + 2) * 128], KSt[:, c * 128:(c + 1) * 128], idb[:]),
                                   inc=("mm", 1))
                        kT_ac = S.add("scalar", lambda slot=slot: nc.scalar.copy(
                            kct[:], pbf[:, slot * 128:(slot + 1) * 128]),
                                      waits=[("mm", t2m), ("ve", st_up)], inc=("ac", 1))
                        kT_ac = S.add("scalar", lambda slot=slot: nc.scalar.copy(
                            kst[:], pbf[:, (slot + 1) * 128:(slot + 2) * 128]), inc=("ac", 1))
                        s1 = S.add("tensor", lambda c=c: nc.tensor.matmul(
                            ps_s1[:, 0:128], kct[:], vdeco[:, c * 128:(c + 1) * 128],
                            start=True, stop=True),
                                   waits=[("ac", kT_ac), ("ve", m_vd)], inc=("mm", 1))
                        s2 = S.add("tensor", lambda c=c: nc.tensor.matmul(
                            ps_s2[:, 0:128], kst[:], vdeco[:, c * 128:(c + 1) * 128],
                            start=True, stop=True), inc=("mm", 1))
                        st_up = S.add("vector", lambda bd=bd: nc.vector.scalar_tensor_tensor(
                            stC[:], stC[:], bd, ps_s1[:, 0:128], ALU.mult, ALU.add),
                                      waits=[("mm", s2)], inc=("ve", 1))
                        st_up = S.add("vector", lambda bd=bd: nc.vector.scalar_tensor_tensor(
                            stS[:], stS[:], bd, ps_s2[:, 0:128], ALU.mult, ALU.add), inc=("ve", 1), sync=True)
                        pbf_free[slot] = ("ac", kT_ac)
                    m_sv = S.add("vector", lambda: nc.vector.tensor_scalar_mul(Scb[:], stC[:], musec[:, 0:1]),
                                 waits=[("ve", st_up)], inc=("ve", 1))
                    m_sv = S.add("vector", lambda: nc.vector.tensor_scalar_mul(Ssb[:], stS[:], musec[:, 0:1]),
                                 inc=("ve", 1), sync=True)
                    m_z2 = S.add("vector", lambda: nc.vector.memset(stC[:], 0.0), inc=("ve", 1))
                    m_z2 = S.add("vector", lambda: nc.vector.memset(stS[:], 0.0), inc=("ve", 1))

                    wq = [("ac", a_tab), ("ld", ld_hd[h]["qh"]), ("ld", ld_hd[h]["kh"]),
                          ("mm", prev_head_mm), ("ve", m_sv)]
                    m_p = S.add("vector", lambda: nc.vector.tensor_mul(QC[:], qh[:], Ctab[:, 0:NT]),
                                waits=wq, inc=("ve", 1))
                    m_p = S.add("vector", lambda: nc.vector.tensor_mul(QS[:], qh[:], Stab[:, 0:NT]), inc=("ve", 1))
                    m_qdr = geo_row(qdrow, 128, s, [("mm", prev_head_mm)])
                    m_qdr = S.add("vector", lambda: nc.vector.tensor_copy(
                        qdrow[0:1, 128:256], qdrow[0:1, 0:128]), inc=("ve", 1), sync=True)
                    m_qdr = S.add("vector", lambda: nc.vector.tensor_copy(
                        qdrow[0:1, 256:512], qdrow[0:1, 0:256]), inc=("ve", 1), sync=True)
                    m_qd = bcast_rows(qdrow, QD512, 512, ps_cc[0],
                                      [("ve", m_qdr), ("ac", vevacs[-1])])
                    m_p2 = None
                    for t in range(TT):
                        m_p2 = S.add("vector", lambda t=t: nc.vector.tensor_mul(
                            QCD[:, t * 512:(t + 1) * 512], QC[:, t * 512:(t + 1) * 512], QD512[:]),
                                     waits=[("ve", m_qd)] if t == 0 else [], inc=("ve", 1))
                        m_p2 = S.add("vector", lambda t=t: nc.vector.tensor_mul(
                            QSD[:, t * 512:(t + 1) * 512], QS[:, t * 512:(t + 1) * 512], QD512[:]),
                                     inc=("ve", 1), sync=True)
                    if "nointer" in dbg:
                        m_p2 = S.add("vector", lambda: nc.vector.memset(QCD[:], 0.0), inc=("ve", 1))
                        m_p2 = S.add("vector", lambda: nc.vector.memset(QSD[:], 0.0), inc=("ve", 1), sync=True)
                    m_qg = None
                    for cb in range(NCH):
                        f = math.exp(-s * 128.0 * cb)
                        sl_ = slice(cb * 128, (cb + 1) * 128)
                        m_qg = S.add("vector", lambda sl_=sl_, f=f: nc.vector.tensor_scalar_mul(
                            qgc[:, sl_], QCD[:, sl_], f), inc=("ve", 1))
                        m_qg = S.add("vector", lambda sl_=sl_, f=f: nc.vector.tensor_scalar_mul(
                            qgs[:, sl_], QSD[:, sl_], f),
                            inc=("ve", 1), sync=(cb == NCH - 1))
                    m_k = S.add("vector", lambda: nc.vector.tensor_mul(KCt[:], kh[:], Ctab[:, 0:NT]), inc=("ve", 1))
                    m_k = S.add("vector", lambda: nc.vector.tensor_mul(KSt[:], kh[:], Stab[:, 0:NT]), inc=("ve", 1), sync=True)
                    vevacs2 = []
                    for g4 in range(4):
                        bank = ps_cc[g4 % 2]
                        wv = [("ld", ld_hd[h]["vh"]), ("ve", m_qd)]
                        if len(vevacs2) >= 2:
                            wv.append(("ac", vevacs2[-2]))
                        ml = None
                        for j in range(4):
                            c = g4 * 4 + j
                            ml = S.add("tensor", lambda bank=bank, j=j, c=c: nc.tensor.transpose(
                                bank[:, j * 128:(j + 1) * 128], vh[:, c * 128:(c + 1) * 128], idf[:]),
                                       waits=(wv if j == 0 else []), inc=("mm", 1))

                        def vev2(bank=bank, g4=g4):
                            ins = None
                            for j in range(4):
                                c = g4 * 4 + j
                                ins = nc.scalar.copy(vtok[:, c * 128:(c + 1) * 128],
                                                     bank[:, j * 128:(j + 1) * 128])
                            return ins
                        vevacs2.append(S.add("scalar", vev2, waits=[("mm", ml)], inc=("ac", 1)))
                    m_vd2 = S.add("vector", lambda h=h: nc.vector.tensor_scalar_mul(
                        vdec[:], vtok[:], kdecc[:, h:h + 1]),
                                  waits=[("ac", vevacs2[-1])], inc=("ve", 1), sync=True)

                    tabstate["prev"] = [("ve", m_k)]
                    if h + 1 < H:
                        ld_hd[h + 1] = head_loads(h + 1, [("ve", m_k), ("ac", vevacs2[-1]),
                                                          ("ve", m_p2)])

                    sc_free = None
                    o_free = None
                    st_up2 = m_zb
                    st_cast = None
                    ev_o = [None] * NCH
                    for c in range(NCH):
                        wsc = [("ve", m_k), ("ve", m_p2)]
                        if sc_free is not None:
                            wsc.append(("ve", sc_free))
                        m1 = S.add("tensor", lambda c=c: nc.tensor.matmul(
                            ps_sc[:, 0:128], KCt[:, c * 128:(c + 1) * 128], QC[:, c * 128:(c + 1) * 128],
                            start=True, stop=False), waits=wsc, inc=("mm", 1))
                        m2 = S.add("tensor", lambda c=c: nc.tensor.matmul(
                            ps_sc[:, 0:128], KSt[:, c * 128:(c + 1) * 128], QS[:, c * 128:(c + 1) * 128],
                            start=False, stop=True), inc=("mm", 1))
                        sc_free = S.add("vector", lambda dT=dT: nc.vector.tensor_mul(scTb[:], ps_sc[:, 0:128], dT),
                                        waits=[("mm", m2), ("gp", ms["p2_pre_gp"])], inc=("ve", 1), sync=True)
                        slot = (c % 4) * 2
                        wv = []
                        if pbf_free[slot] is not None:
                            wv.append(pbf_free[slot])
                        t1 = S.add("tensor", lambda slot=slot, c=c: nc.tensor.transpose(
                            pbf[:, slot * 128:(slot + 1) * 128], KCt[:, c * 128:(c + 1) * 128], idb[:]),
                                   waits=wv, inc=("mm", 1))
                        t2m = S.add("tensor", lambda slot=slot, c=c: nc.tensor.transpose(
                            pbf[:, (slot + 1) * 128:(slot + 2) * 128], KSt[:, c * 128:(c + 1) * 128], idb[:]),
                                   inc=("mm", 1))
                        kT_ac = S.add("scalar", lambda slot=slot: nc.scalar.copy(
                            kct[:], pbf[:, slot * 128:(slot + 1) * 128]),
                                      waits=[("mm", t2m), ("ve", st_up2)], inc=("ac", 1))
                        kT_ac = S.add("scalar", lambda slot=slot: nc.scalar.copy(
                            kst[:], pbf[:, (slot + 1) * 128:(slot + 2) * 128]), inc=("ac", 1))
                        pbf_free[slot] = ("ac", kT_ac)
                        wo = [("ve", sc_free), ("ac", st_cast if c > 0 else None)]
                        wo = [(a, b) for a, b in wo if b is not None]
                        if o_free is not None:
                            wo.append(("ac", o_free))
                        else:
                            wo.append(("ac", vevacs2[-1]))
                        o1 = S.add("tensor", lambda c=c: nc.tensor.matmul(
                            ps_o[:, 0:128], vtok[:, c * 128:(c + 1) * 128], scTb[:],
                            start=True, stop=False), waits=wo, inc=("mm", 1))
                        o2 = S.add("tensor", lambda c=c: nc.tensor.matmul(
                            ps_o[:, 0:128], stCb[:], QCD[:, c * 128:(c + 1) * 128],
                            start=False, stop=False), inc=("mm", 1))
                        o3 = S.add("tensor", lambda c=c: nc.tensor.matmul(
                            ps_o[:, 0:128], stSb[:], QSD[:, c * 128:(c + 1) * 128],
                            start=False, stop=True), inc=("mm", 1))
                        s1 = S.add("tensor", lambda c=c: nc.tensor.matmul(
                            ps_s1[:, 0:128], kct[:], vdec[:, c * 128:(c + 1) * 128],
                            start=True, stop=True),
                                   waits=[("ac", kT_ac), ("ve", m_vd2)], inc=("mm", 1))
                        s2 = S.add("tensor", lambda c=c: nc.tensor.matmul(
                            ps_s2[:, 0:128], kst[:], vdec[:, c * 128:(c + 1) * 128],
                            start=True, stop=True), inc=("mm", 1))
                        wev = [("mm", o3)]
                        if h > 0 and prev_spill_st is not None and c == 0:
                            wev.append(("st", prev_spill_st))
                        ev_o[c] = S.add("scalar", lambda c=c: nc.scalar.copy(
                            o_h[:, c * 128:(c + 1) * 128], ps_o[:, 0:128]), waits=wev, inc=("ac", 1))
                        o_free = ev_o[c]
                        st_up2 = S.add("vector", lambda bd=bd: nc.vector.scalar_tensor_tensor(
                            stC[:], stC[:], bd, ps_s1[:, 0:128], ALU.mult, ALU.add),
                                       waits=[("mm", s2)], inc=("ve", 1))
                        st_up2 = S.add("vector", lambda bd=bd: nc.vector.scalar_tensor_tensor(
                            stS[:], stS[:], bd, ps_s2[:, 0:128], ALU.mult, ALU.add), inc=("ve", 1), sync=True)
                        st_cast = S.add("scalar", lambda: nc.scalar.copy(stCb[:], stC[:]),
                                        waits=[("ve", st_up2)], inc=("ac", 1))
                        st_cast = S.add("scalar", lambda: nc.scalar.copy(stSb[:], stS[:]), inc=("ac", 1))

                    corr_ve = None
                    for t in range(TT):
                        bank = ps_cc[t % 2]
                        wcc = [("ve", m_sv), ("ve", m_qg), ("ac", ev_o[4 * t + 3]),
                               ("ac", vevacs2[-1])]
                        if corr_ve is not None and t >= 2:
                            wcc.append(("ve", corr_ve))
                        c1m = S.add("tensor", lambda bank=bank, t=t: nc.tensor.matmul(
                            bank[:, 0:512], Scb[:], qgc[:, t * 512:(t + 1) * 512],
                            start=True, stop=False), waits=wcc, inc=("mm", 1))
                        c2m = S.add("tensor", lambda bank=bank, t=t: nc.tensor.matmul(
                            bank[:, 0:512], Ssb[:], qgs[:, t * 512:(t + 1) * 512],
                            start=False, stop=True), inc=("mm", 1))
                        corr_ve = S.add("vector", lambda bank=bank, t=t: nc.vector.tensor_add(
                            o_h[:, t * 512:(t + 1) * 512], o_h[:, t * 512:(t + 1) * 512], bank[:, 0:512]),
                                        waits=[("mm", c2m)], inc=("ve", 1), sync=True)
                    prev_spill_st = S.add("scalar", lambda h=h: nc.scalar.dma_start(
                        out=oT_scr[h * 128:(h + 1) * 128, :], in_=o_h[:]),
                        waits=[("ve", corr_ve)], inc=("st", 16))
                    prev_head_mm = S.planned["mm"]
                    prev_corr_ve = corr_ve

                ms["p2_st"] = prev_spill_st
                ms["p2_mm"] = prev_head_mm


        if "p3" in phases:
            p3 = contextlib.ExitStack()
            with p3:
                ps3 = [p3.enter_context(nc.psum_tensor(f"p3ps{i}", [128, 512], F32)) for i in range(8)]
                p3b = contextlib.ExitStack()
                p3b.__enter__()
                sb3 = lambda n, shp, dt: p3b.enter_context(nc.sbuf_tensor(n, shp, dt))
                oT = sb3("oT", [128, KK * NT], F32R)
                wpb = [sb3(f"wpb{i}", [128, KK * 128], F32R) for i in range(2)]
                xrow = [sb3(f"xrow{i}", [128, NT], F32) for i in range(2)]
                trow = [sb3(f"trow{i}", [128, NT], F32) for i in range(2)]

                S.fence()
                m_oT = None
                for kk in range(KK):
                    m_oT = S.add("sync", lambda kk=kk: nc.sync.dma_start(
                        out=oT[:, kk * NT:(kk + 1) * NT], in_=oT_scr[kk * 128:(kk + 1) * 128, :].bitcast(F32R)),
                        inc=("ld", 16))

                bank_free3 = [None] * 8
                blk_mm = [None] * 16
                ev_ve = [None] * 16
                sp_st = [None] * 16
                xld_ms = [None] * 16
                for ob in range(16):
                    w = [("ld", m_oT)] if ob == 0 else []
                    if ob >= 2:
                        w.append(("mm", blk_mm[ob - 2]))
                    wl = S.add("sync", lambda ob=ob: nc.sync.dma_start(out=wpb[ob % 2][:], in_=wproj[ob]),
                               waits=w, inc=("ld", 16))
                    w2 = [("ve", ev_ve[ob - 2])] if ob >= 2 else []
                    xld_ms[ob] = S.add("sync", lambda ob=ob: nc.sync.dma_start(
                        out=xrow[ob % 2][:], in_=xT_scr[ob * 128:(ob + 1) * 128, :]),
                        waits=w2, inc=("ld", 16))
                    mlast = None
                    for kk in range(KK):
                        for tt in range(TT):
                            b = (ob % 2) * 4 + tt
                            tw = []
                            if kk == 0:
                                tw = [("ld", wl)]
                                if bank_free3[b] is not None:
                                    tw.append(("ve", bank_free3[b]))
                            mlast = S.add("tensor", lambda b=b, kk=kk, tt=tt, ob=ob:
                                          nc.tensor.matmul(ps3[b][:, 0:512],
                                                           wpb[ob % 2][:, kk * 128:(kk + 1) * 128],
                                                           oT[:, kk * NT + tt * 512: kk * NT + (tt + 1) * 512],
                                                           start=(kk == 0), stop=(kk == KK - 1)),
                                          waits=tw,
                                          inc=("mm", 1) if (kk == KK - 1 and tt == TT - 1) else None)
                    blk_mm[ob] = S.planned["mm"]

                    def ev(ob=ob):
                        ins = None
                        for tt in range(TT):
                            b = (ob % 2) * 4 + tt
                            ins = nc.vector.tensor_add(trow[ob % 2][:, tt * 512:(tt + 1) * 512],
                                                       ps3[b][:, 0:512],
                                                       xrow[ob % 2][:, tt * 512:(tt + 1) * 512])
                        return ins
                    wv = [("mm", blk_mm[ob]), ("ld", xld_ms[ob])]
                    if ob >= 2:
                        wv.append(("st", sp_st[ob - 2]))
                    ev_ve[ob] = S.add("vector", ev, waits=wv, inc=("ve", 1))
                    for tt in range(TT):
                        bank_free3[(ob % 2) * 4 + tt] = ev_ve[ob]
                    sp_st[ob] = S.add("scalar", lambda ob=ob: nc.scalar.dma_start(
                        out=tmp_scr[ob * 128:(ob + 1) * 128, :], in_=trow[ob % 2][:]),
                        waits=[("ve", ev_ve[ob])], inc=("st", 16))
                m_p3b_st = sp_st[15]
                p3b.__exit__(None, None, None)
                p36 = contextlib.ExitStack()
                es.enter_context(p36)
                x1b = p36.enter_context(nc.sbuf_tensor("x1b", [128, KK * NT], BF16))
                acc = p36.enter_context(nc.sbuf_tensor("acc", [128, KK * NT], BF16))
                sb3c = lambda n, shp, dt: p3.enter_context(nc.sbuf_tensor(n, shp, dt))
                srowlog = es.enter_context(nc.sbuf_tensor("srowlog", [128, NCH * 8], F32)) if "srow" in dbg else None
                tcol = [sb3c(f"tcol{i}", [128, KK * 128], F32) for i in range(2)]
                ttok = sb3c("ttok", [128, D], F32)
                sqt = sb3c("sqt", [128, D], F32)
                xtok = sb3c("xtok", [128, D], F32)
                srow = sb3c("srow", [128, 32], F32)
                S.fence()

                tmp_v = tmp_scr.rearrange("(kk p) t -> kk p t", p=128)
                tg_ld = [None] * NCH
                tg_fwdmm = [None] * NCH
                tg_btmm = [None] * NCH
                tg_done_ve = [None] * NCH
                prev_bt_mm = None
                prev_ev2 = None
                prev_fwd_ve = None
                prev_apply_ve = None
                for tg in range(NCH):
                    buf = tcol[tg % 2]
                    w = []
                    if tg == 0:
                        w = [("st", m_p3b_st)]
                    if tg >= 2:
                        w.append(("mm", tg_fwdmm[tg - 2]))
                    lm = None
                    for kk in range(KK):
                        wk = list(w) if kk == 0 else []
                        lm = S.add("sync", lambda kk=kk, tg=tg, buf=buf: nc.sync.dma_start(
                            out=buf[:, kk * 128:(kk + 1) * 128], in_=tmp_v[kk, :, tg * 128:(tg + 1) * 128]),
                            waits=wk, inc=("ld", 16))
                    tg_ld[tg] = lm
                    fwd_ve = None
                    for g4 in range(4):
                        bank = ps3[g4 % 2]
                        wv2 = [("ld", lm)]
                        if prev_fwd_ve is not None and g4 < 2:
                            wv2.append(("ve", prev_fwd_ve))
                        if g4 >= 2 and fwd_ve is not None:
                            wv2.append(("ve", fwd_ve))
                        ml = None
                        for j in range(4):
                            kk = g4 * 4 + j
                            ml = S.add("tensor", lambda bank=bank, j=j, kk=kk, buf=buf: nc.tensor.transpose(
                                bank[:, j * 128:(j + 1) * 128], buf[:, kk * 128:(kk + 1) * 128], idf[:]),
                                       waits=(wv2 if j == 0 else []), inc=("mm", 1))

                        def fev(bank=bank, g4=g4):
                            ins = None
                            for j in range(4):
                                kk = g4 * 4 + j
                                ins = nc.vector.tensor_copy(ttok[:, kk * 128:(kk + 1) * 128],
                                                            bank[:, j * 128:(j + 1) * 128])
                            return ins
                        wf = [("mm", ml)]
                        if prev_apply_ve is not None:
                            wf.append(("ve", prev_apply_ve))
                        fwd_ve = S.add("vector", fev, waits=wf, inc=("ve", 1), sync=(g4 == 3))
                        tg_fwdmm[tg] = ml
                    prev_fwd_ve = fwd_ve
                    m_s1 = S.add("vector", lambda: nc.vector.reduce_sum(srow[:, 0:1], ttok[:], axis=AX.X),
                                 waits=[("ve", fwd_ve)], inc=("ve", 1), sync=True)
                    m_s2 = S.add("scalar", lambda: nc.scalar.mul(srow[:, 4:5], srow[:, 0:1], consts[:, 11:12]),
                                 waits=[("ve", m_s1)], inc=("ac", 1))
                    m_a1 = S.add("vector", lambda: nc.vector.tensor_scalar_sub(ttok[:], ttok[:], srow[:, 4:5]),
                                 waits=[("ac", m_s2)], inc=("ve", 1), sync=True)
                    m_sq = S.add("scalar", lambda: nc.scalar.activation(sqt[:], ttok[:], AF.Square),
                                 waits=[("ve", m_a1)], inc=("ac", 1))
                    m_s3 = S.add("vector", lambda: nc.vector.reduce_sum(srow[:, 8:9], sqt[:], axis=AX.X),
                                 waits=[("ac", m_sq)], inc=("ve", 1), sync=True)
                    m_s4 = S.add("scalar", lambda: nc.scalar.mul(srow[:, 16:17], srow[:, 8:9], consts[:, 11:12]),
                                 waits=[("ve", m_s3)], inc=("ac", 1))
                    m_s5 = S.add("scalar", lambda: nc.scalar.add(srow[:, 16:17], srow[:, 16:17], consts[:, 1:2]),
                                 inc=("ac", 1))
                    m_ln = S.add("scalar", lambda: nc.scalar.activation(srow[:, 20:21], srow[:, 16:17], AF.Ln),
                                 inc=("ac", 1))
                    m_lh = S.add("scalar", lambda: nc.scalar.mul(srow[:, 28:29], srow[:, 20:21], consts[:, 4:5]),
                                 inc=("ac", 1))
                    m_rs = S.add("scalar", lambda: nc.scalar.activation(srow[:, 24:25], srow[:, 28:29], AF.Exp),
                                 inc=("ac", 1))
                    wa = [("ac", m_rs)]
                    if prev_bt_mm is not None:
                        wa.append(("mm", prev_bt_mm))
                    m_a2 = S.add("vector", lambda: nc.vector.tensor_scalar_mul(xtok[:], ttok[:], srow[:, 24:25]),
                                 waits=wa, inc=("ve", 1), sync=True)
                    prev_apply_ve = m_a2
                    if srowlog is not None:
                        m_a2 = S.add("vector", lambda tg=tg: nc.vector.tensor_copy(
                            srowlog[:, tg * 8:(tg + 1) * 8],
                            srow[:].rearrange("p (a b) -> p a b", b=4)[:, :, 0:1].rearrange("p a b -> p (a b)")),
                            inc=("ve", 1))
                        prev_apply_ve = m_a2
                    ev2 = None
                    for g4 in range(4):
                        bank = ps3[2 + g4 % 2]
                        wv3 = [("ve", m_a2)]
                        if prev_ev2 is not None and g4 < 2:
                            wv3.append(("ve", prev_ev2))
                        if g4 >= 2 and ev2 is not None:
                            wv3.append(("ve", ev2))
                        ml = None
                        for j in range(4):
                            kk = g4 * 4 + j
                            ml = S.add("tensor", lambda bank=bank, j=j, kk=kk: nc.tensor.transpose(
                                bank[:, j * 128:(j + 1) * 128], xtok[:, kk * 128:(kk + 1) * 128], idf[:]),
                                       waits=(wv3 if j == 0 else []), inc=("mm", 1))

                        def bev(bank=bank, g4=g4, tg=tg):
                            ins = None
                            for j in range(4):
                                kk = g4 * 4 + j
                                dst = slice(kk * NT + tg * 128, kk * NT + (tg + 1) * 128)
                                nc.vector.tensor_copy(x1b[:, dst], bank[:, j * 128:(j + 1) * 128])
                                ins = nc.vector.tensor_scalar_mul(acc[:, dst], bank[:, j * 128:(j + 1) * 128], 2.0)
                            return ins
                        ev2 = S.add("vector", bev, waits=[("mm", ml)], inc=("ve", 1))
                        tg_btmm[tg] = ml
                    prev_bt_mm = tg_btmm[tg]
                    prev_ev2 = ev2
                    tg_done_ve[tg] = ev2
                ms["p3_ve"] = tg_done_ve[NCH - 1]


        if "p4" in phases:
            p4 = contextlib.ExitStack()
            comb = es.enter_context(nc.sbuf_tensor("comb", [128, NCH * 8], F32))
            p4log = es.enter_context(nc.sbuf_tensor("p4log", [128, 8 * 32], F32)) if "p4log" in dbg else None
            with p4:
                sb4 = lambda n, shp, dt: p4.enter_context(nc.sbuf_tensor(n, shp, dt))
                wrts = sb4("wrts", [128, KK * 8], F32)
                wrtb = sb4("wrtb", [128, KK * 8], BF16)
                lg = sb4("lg", [128, 8], F32)
                el = sb4("el", [128, 8], F32)
                wts = sb4("wts", [128, 8], F32)
                wre = sb4("wre", [128, 8], F32)
                msk = sb4("msk", [128, 8], F32)
                rcol = sb4("rcol", [128, 12], F32)
                ps4 = [p4.enter_context(nc.psum_tensor(f"p4ps{i}", [128, 512], F32)) for i in range(2)]
                S.fence()
                m_w = S.add("sync", lambda: nc.sync.dma_start(out=wrts[:], in_=wrt), inc=("ld", 16))
                S.add("vector", lambda: nc.vector.memset(rcol[:], 0.0), inc=("ve", 1))
                m_wb = S.add("vector", lambda: nc.vector.tensor_copy(wrtb[:], wrts[:]),
                             waits=[("ld", m_w)], inc=("ve", 1))
                prev_lg_ve = None
                for tg in range(NCH):
                    bank = ps4[tg % 2]
                    wt0 = [("ve", m_wb)]
                    if prev_lg_ve is not None and tg >= 2:
                        wt0.append(("ve", prev_lg_ve))
                    ml = None
                    for kk in range(KK):
                        ml = S.add("tensor", lambda bank=bank, kk=kk, tg=tg: nc.tensor.matmul(
                            bank[:, 0:8], x1b[:, kk * NT + tg * 128: kk * NT + (tg + 1) * 128],
                            wrtb[:, kk * 8:(kk + 1) * 8],
                            start=(kk == 0), stop=(kk == KK - 1)),
                            waits=(wt0 if kk == 0 else []),
                            inc=("mm", 1) if kk == KK - 1 else None)
                    mv = S.add("vector", lambda bank=bank: nc.vector.tensor_copy(lg[:], bank[:, 0:8]),
                               waits=[("mm", ml)], inc=("ve", 1), sync=True)
                    prev_lg_ve = mv
                    r1 = S.add("vector", lambda: nc.vector.reduce_max(rcol[:, 0:1], lg[:], axis=AX.X), inc=("ve", 1), sync=True)
                    bb = S.add("scalar", lambda: nc.scalar.mul(rcol[:, 1:2], rcol[:, 0:1], -1.0),
                               waits=[("ve", r1)], inc=("ac", 1))
                    ma = S.add("scalar", lambda: nc.scalar.activation(el[:], lg[:], AF.Exp, bias=rcol[:, 1:2]),
                               inc=("ac", 1))
                    r2 = S.add("vector", lambda: nc.vector.reduce_sum(rcol[:, 3:4], el[:], axis=AX.X),
                               waits=[("ac", ma)], inc=("ve", 1), sync=True)
                    bs = S.add("scalar", lambda: nc.scalar.copy(rcol[:, 4:5], rcol[:, 3:4]),
                               waits=[("ve", r2)], inc=("ac", 1))
                    mv = S.add("vector", lambda: nc.vector.reciprocal(rcol[:, 9:10], rcol[:, 4:5]),
                               waits=[("ac", bs)], inc=("ve", 1), sync=True)
                    br = S.add("scalar", lambda: nc.scalar.copy(rcol[:, 10:11], rcol[:, 9:10]),
                               waits=[("ve", mv)], inc=("ac", 1))
                    mv = S.add("vector", lambda: nc.vector.tensor_scalar_mul(wts[:], el[:], rcol[:, 10:11]),
                               waits=[("ac", br)], inc=("ve", 1), sync=True)
                    r3 = S.add("vector", lambda: nc.vector.reduce_max(rcol[:, 5:6], wts[:], axis=AX.X), inc=("ve", 1), sync=True)
                    bm = S.add("scalar", lambda: nc.scalar.copy(rcol[:, 6:7], rcol[:, 5:6]),
                               waits=[("ve", r3)], inc=("ac", 1))
                    mv = S.add("vector", lambda: nc.vector.tensor_single_scalar(msk[:], wts[:], rcol[:, 6:7],
                                                                                ALU.is_equal),
                               waits=[("ac", bm)], inc=("ve", 1), sync=True)
                    mv = S.add("vector", lambda: nc.vector.scalar_tensor_tensor(wre[:], msk[:], -1e30, wts[:],
                                                                                ALU.mult, ALU.add), inc=("ve", 1), sync=True)
                    r4 = S.add("vector", lambda: nc.vector.reduce_max(rcol[:, 7:8], wre[:], axis=AX.X), inc=("ve", 1), sync=True)
                    bm2 = S.add("scalar", lambda: nc.scalar.copy(rcol[:, 8:9], rcol[:, 7:8]),
                                waits=[("ve", r4)], inc=("ac", 1))
                    mv = S.add("vector", lambda: nc.vector.tensor_single_scalar(msk[:], wts[:], rcol[:, 8:9],
                                                                                ALU.is_ge),
                               waits=[("ac", bm2)], inc=("ve", 1), sync=True)
                    mv = S.add("vector", lambda tg=tg: nc.vector.tensor_mul(comb[:, tg * 8:(tg + 1) * 8],
                                                                            wts[:], msk[:]), inc=("ve", 1), sync=True)
                    if p4log is not None and tg < 8:
                        S.add("vector", lambda tg=tg: nc.vector.tensor_copy(
                            p4log[:, tg * 32:tg * 32 + 8], lg[:]), inc=("ve", 1))
                        S.add("vector", lambda tg=tg: nc.vector.tensor_copy(
                            p4log[:, tg * 32 + 8:tg * 32 + 16], el[:]), inc=("ve", 1))
                        S.add("vector", lambda tg=tg: nc.vector.tensor_copy(
                            p4log[:, tg * 32 + 16:tg * 32 + 24], wts[:]), inc=("ve", 1))
                        mv = S.add("vector", lambda tg=tg: nc.vector.tensor_copy(
                            p4log[:, tg * 32 + 24:tg * 32 + 32], rcol[:, 0:12:2].rearrange("p a -> p a") if False else rcol[:, 0:8]), inc=("ve", 1))
                ms["p4_ve"] = mv

        if "p5" in phases:
            p5 = contextlib.ExitStack()
            with p5:
                sb5 = lambda n, shp, dt: p5.enter_context(nc.sbuf_tensor(n, shp, dt))
                hbuf = sb5("hbuf", [128, NII * NT], BF16)
                gu_f0 = sb5("guf0", [128, NT // 2], F32)
                gu_f1 = sb5("guf1", [128, NT // 2], F32)
                gu_b = [sb5(f"gub{i}", [128, NT], BF16) for i in range(2)]
                dn_f = sb5("dnf", [128, FF // 2], F32)
                dn_b = [sb5(f"dnb{i}", [128, FF], BF16) for i in range(2)]
                combB = sb5("combB", [128, NT], BF16)
                crow = sb5("crow", [1, NT // 4], F32R)
                sigb = sb5("sigb", [128, 512], F32)
                ps5 = [p5.enter_context(nc.psum_tensor(f"p5ps{i}", [128, 512], F32)) for i in range(8)]
                S.fence()

                path_down_mm = None
                prev_gu_mm = {}
                prev_acc_ve = None
                prev_comb_ve = None
                for p_i in range(NPATH):
                    if p_i >= 1:
                        e = p_i - 1
                        bve = None
                        for hh in range(4):
                            cve = None
                            for tgl in range(4):
                                tg = hh * 4 + tgl
                                bank = ps5[6 + (tg % 2)]
                                wtc = [("ve", ms["p4_ve"])]
                                if prev_acc_ve is not None:
                                    wtc.append(("ve", prev_acc_ve))
                                if prev_comb_ve is not None:
                                    wtc.append(("ve", prev_comb_ve))
                                if bve is not None and tgl == 0:
                                    wtc.append(("ve", bve))
                                if cve is not None and tgl >= 2:
                                    wtc.append(("ve", cve))
                                mt = S.add("tensor", lambda bank=bank, tg=tg, e=e: nc.tensor.transpose(
                                    bank[0:1, 0:128], comb[:, tg * 8 + e: tg * 8 + e + 1], idf[:]),
                                    waits=wtc, inc=("mm", 1))
                                cve = S.add("vector", lambda bank=bank, tgl=tgl: nc.vector.tensor_copy(
                                    crow[0:1, tgl * 128:(tgl + 1) * 128], bank[0:1, 0:128]),
                                    waits=[("mm", mt)], inc=("ve", 1), sync=True)
                            bank = ps5[6 + (hh % 2)]
                            mt = S.add("tensor", lambda bank=bank: nc.tensor.matmul(
                                bank[:, 0:512], ones1[:], crow[0:1, 0:512],
                                start=True, stop=True), waits=[("ve", cve)], inc=("mm", 1))
                            bve = S.add("vector", lambda bank=bank, hh=hh: nc.vector.tensor_copy(
                                combB[:, hh * 512:(hh + 1) * 512], bank[:, 0:512]),
                                waits=[("mm", mt)], inc=("ve", 1), sync=True)
                        prev_comb_ve = bve
                        m_cB = bve
                    else:
                        m_cB = None

                    unit_ev = {}
                    qcast = {}
                    hav = {}
                    last_unit_mm = {}
                    qfree = {}
                    prev_cast2 = {0: None, 1: None}
                    for ii in range(NII):
                        for q in range(4):
                            half, qq = q // 2, q % 2
                            lane = q // 2
                            buf = gu_f0 if lane == 0 else gu_f1
                            wld = []
                            if prev_cast2[lane] is not None:
                                wld.append(("ve", prev_cast2[lane]))
                            if ii >= 1 and (ii - 1, q) in qfree:
                                wld.append(("mm", qfree[(ii - 1, q)]))
                            elif ii == 0:
                                if path_down_mm is not None:
                                    wld.append(("mm", path_down_mm))
                                if prev_acc_ve is not None:
                                    wld.append(("ve", prev_acc_ve))
                            lm = S.add("sync", lambda p_i=p_i, ii=ii, half=half, qq=qq, buf=buf:
                                       nc.sync.dma_start(out=buf[:],
                                                         in_=wgu[p_i, ii, half][:, qq * 1024:(qq + 1) * 1024]),
                                       waits=wld, inc=("ld", 16))
                            wcast = [("ld", lm)]
                            prev_cast2[lane] = S.add("vector", lambda half=half, qq=qq, buf=buf:
                                                     nc.vector.tensor_copy(
                                gu_b[half][:, qq * 1024:(qq + 1) * 1024], buf[:]),
                                waits=wcast, inc=("ve", 1), sync=True)
                            qcast[(ii, q)] = prev_cast2[lane]
                        for u2 in range(2):
                            bs = 0 if (ii * 2 + u2) % 2 == 0 else 4
                            banks = [ps5[bs], ps5[bs + 1], ps5[bs + 2], ps5[bs + 3]]
                            mlast = None
                            for kk in range(KK):
                                half = kk // 8
                                kk8 = kk % 8
                                tw = []
                                if kk % 4 == 0:
                                    tw = [("ve", qcast[(ii, kk // 4)])]
                                    if kk == 0:
                                        uk = (ii * 2 + u2 - 2)
                                        if uk >= 0:
                                            pii, pu2 = uk // 2, uk % 2
                                            if (pii, pu2) in unit_ev:
                                                tw.append(("ve", unit_ev[(pii, pu2)]))
                                for gi in range(2):
                                    for ti in range(2):
                                        tt = u2 * 2 + ti
                                        b = banks[gi * 2 + ti]
                                        want_inc = (gi == 1 and ti == 1 and kk % 4 == 3)
                                        mv = S.add("tensor",
                                                   lambda b=b, kk8=kk8, half=half, gi=gi, tt=tt, kk=kk:
                                                   nc.tensor.matmul(
                                                       b[:, 0:512],
                                                       gu_b[half][:, kk8 * 256 + gi * 128: kk8 * 256 + gi * 128 + 128],
                                                       x1b[:, kk * NT + tt * 512: kk * NT + (tt + 1) * 512],
                                                       start=(kk == 0), stop=(kk == KK - 1)),
                                                   waits=(tw if (gi == 0 and ti == 0) else []),
                                                   inc=("mm", 1) if want_inc else None)
                                        if want_inc and u2 == 1:
                                            qfree[(ii, kk // 4)] = S.planned["mm"]
                            last_unit_mm[(ii, u2)] = S.planned["mm"]
                            wsg = [("mm", last_unit_mm[(ii, u2)])]
                            uk = (ii * 2 + u2 - 1)
                            if uk >= 0:
                                pii, pu2 = uk // 2, uk % 2
                                if (pii, pu2) in unit_ev:
                                    wsg.append(("ve", unit_ev[(pii, pu2)]))

                            uev_m = None
                            for ti in range(2):
                                tt = u2 * 2 + ti
                                dsl = slice(ii * NT + tt * 512, ii * NT + (tt + 1) * 512)
                                wsg_t = list(wsg) if ti == 0 else []
                                if ti == 1 or uev_m is not None:
                                    wsg_t.append(("ve", uev_m if uev_m is not None else 0))
                                wsg_t = [(a2, b2) for a2, b2 in wsg_t if b2]
                                a_sig = S.add("scalar", lambda banks=banks, ti=ti: nc.scalar.activation(
                                    sigb[:, 0:512], banks[ti][:, 0:512], AF.Sigmoid),
                                    waits=wsg_t, inc=("ac", 1))
                                wu = [("ac", a_sig)]
                                if ti == 0:
                                    if path_down_mm is not None:
                                        wu.append(("mm", path_down_mm))
                                    if m_cB is not None:
                                        wu.append(("ve", m_cB))
                                if p_i >= 1:
                                    S.add("vector", lambda dsl=dsl, banks=banks, ti=ti: nc.vector.tensor_mul(
                                        hbuf[:, dsl], sigb[:, 0:512], banks[2 + ti][:, 0:512]),
                                        waits=wu, inc=("ve", 1))
                                    uev_m = S.add("vector", lambda dsl=dsl, tt=tt: nc.vector.tensor_mul(
                                        hbuf[:, dsl], hbuf[:, dsl], combB[:, tt * 512:(tt + 1) * 512]),
                                        inc=("ve", 1), sync=True)
                                else:
                                    uev_m = S.add("vector", lambda dsl=dsl, banks=banks, ti=ti: nc.vector.tensor_mul(
                                        hbuf[:, dsl], sigb[:, 0:512], banks[2 + ti][:, 0:512]),
                                        waits=wu, inc=("ve", 1), sync=True)
                            unit_ev[(ii, u2)] = uev_m
                            hav[(ii, u2)] = uev_m

                    dn_last_mm = None
                    dn_ev = {}
                    dcast = {}
                    for of in range(16):
                        wld = []
                        if of >= 1:
                            wld.append(("ve", dcast[of - 1]))
                        if of >= 2 and (of - 2) in dn_ev:
                            wld.append(("ve", dn_ev[of - 2]))
                        HF = FF // 2
                        lm = S.add("sync", lambda p_i=p_i, of=of: nc.sync.dma_start(
                            out=dn_f[:], in_=wdn[p_i, of][:, 0:HF]), waits=wld, inc=("ld", 16))
                        dc0 = S.add("vector", lambda of=of: nc.vector.tensor_copy(
                            dn_b[of % 2][:, 0:HF], dn_f[:]), waits=[("ld", lm)], inc=("ve", 1), sync=True)
                        lm2 = S.add("sync", lambda p_i=p_i, of=of: nc.sync.dma_start(
                            out=dn_f[:], in_=wdn[p_i, of][:, HF:FF]), waits=[("ve", dc0)], inc=("ld", 16))
                        dcast[of] = S.add("vector", lambda of=of: nc.vector.tensor_copy(
                            dn_b[of % 2][:, HF:FF], dn_f[:]), waits=[("ld", lm2)], inc=("ve", 1), sync=True)
                        mlast = None
                        for ii in range(NII):
                            for tt in range(TT):
                                b = ps5[(of % 2) * 4 + tt]
                                tw = []
                                if ii == 0 and tt == 0:
                                    tw = [("ve", dcast[of]), ("ve", hav[(NII - 1, 1)])]
                                    if of >= 2 and (of - 2) in dn_ev:
                                        tw.append(("ve", dn_ev[of - 2]))
                                mlast = S.add("tensor", lambda b=b, of=of, ii=ii, tt=tt: nc.tensor.matmul(
                                    b[:, 0:512], dn_b[of % 2][:, ii * 128:(ii + 1) * 128],
                                    hbuf[:, ii * NT + tt * 512: ii * NT + (tt + 1) * 512],
                                    start=(ii == 0), stop=(ii == NII - 1)),
                                    waits=tw, inc=("mm", 1) if (ii == NII - 1 and tt == TT - 1) else None)
                        dn_last_mm = S.planned["mm"]

                        def dev(of=of):
                            ins = None
                            for tt in range(TT):
                                b = ps5[(of % 2) * 4 + tt]
                                dsl = slice(of * NT + tt * 512, of * NT + (tt + 1) * 512)
                                ins = nc.vector.tensor_add(acc[:, dsl], acc[:, dsl], b[:, 0:512])
                            return ins
                        wd = [("mm", dn_last_mm)]
                        dn_ev[of] = S.add("vector", dev, waits=wd, inc=("ve", 1))
                    path_down_mm = dn_last_mm
                    prev_acc_ve = dn_ev[15]
                ms["p5_ve"] = prev_acc_ve
                ms["p5_mm"] = path_down_mm

        if "p6" in phases:
            p6 = contextlib.ExitStack()
            with p6:
                sb6 = lambda n, shp, dt: p6.enter_context(nc.sbuf_tensor(n, shp, dt))
                atok = sb6("atok", [128, D], F32)
                sq6 = sb6("sq6", [128, D], F32)
                otok = [sb6(f"otok{i}", [128, D], F32) for i in range(2)]
                sr6 = sb6("sr6", [128, 32], F32)
                pb6 = [p6.enter_context(nc.psum_tensor(f"p6ps{i}", [128, 1024], BF16)) for i in range(2)]
                S.fence()
                prev_apply6 = None
                out_st = None
                prev_fwd6 = None
                for tg in range(NCH):
                    fwd_ve = None
                    for g8 in range(2):
                        bank = pb6[g8]
                        wv = []
                        if prev_fwd6 is not None:
                            wv.append(("ve", prev_fwd6))
                        ml = None
                        for j in range(8):
                            kk = g8 * 8 + j
                            ml = S.add("tensor", lambda bank=bank, j=j, kk=kk, tg=tg: nc.tensor.transpose(
                                bank[:, j * 128:(j + 1) * 128],
                                acc[:, kk * NT + tg * 128: kk * NT + (tg + 1) * 128], idb[:]),
                                waits=(wv if j == 0 else []), inc=("mm", 1))

                        def fev6(bank=bank, g8=g8):
                            ins = None
                            for j in range(8):
                                kk = g8 * 8 + j
                                ins = nc.vector.tensor_copy(atok[:, kk * 128:(kk + 1) * 128],
                                                            bank[:, j * 128:(j + 1) * 128])
                            return ins
                        wf = [("mm", ml)]
                        if prev_apply6 is not None:
                            wf.append(("ve", prev_apply6))
                        fwd_ve = S.add("vector", fev6, waits=wf, inc=("ve", 1), sync=(g8 == 1))
                    prev_fwd6 = fwd_ve
                    m_s1 = S.add("vector", lambda: nc.vector.reduce_sum(sr6[:, 0:1], atok[:], axis=AX.X),
                                 waits=[("ve", fwd_ve)], inc=("ve", 1), sync=True)
                    m_s2 = S.add("scalar", lambda: nc.scalar.mul(sr6[:, 4:5], sr6[:, 0:1], consts[:, 11:12]),
                                 waits=[("ve", m_s1)], inc=("ac", 1))
                    m_a1 = S.add("vector", lambda: nc.vector.tensor_scalar_sub(atok[:], atok[:], sr6[:, 4:5]),
                                 waits=[("ac", m_s2)], inc=("ve", 1), sync=True)
                    m_sq = S.add("scalar", lambda: nc.scalar.activation(sq6[:], atok[:], AF.Square),
                                 waits=[("ve", m_a1)], inc=("ac", 1))
                    m_s3 = S.add("vector", lambda: nc.vector.reduce_sum(sr6[:, 8:9], sq6[:], axis=AX.X),
                                 waits=[("ac", m_sq)], inc=("ve", 1), sync=True)
                    m_s4 = S.add("scalar", lambda: nc.scalar.mul(sr6[:, 16:17], sr6[:, 8:9], consts[:, 11:12]),
                                 waits=[("ve", m_s3)], inc=("ac", 1))
                    m_s5 = S.add("scalar", lambda: nc.scalar.add(sr6[:, 16:17], sr6[:, 16:17], consts[:, 1:2]),
                                 inc=("ac", 1))
                    m_ln = S.add("scalar", lambda: nc.scalar.activation(sr6[:, 20:21], sr6[:, 16:17], AF.Ln),
                                 inc=("ac", 1))
                    m_lh = S.add("scalar", lambda: nc.scalar.mul(sr6[:, 28:29], sr6[:, 20:21], consts[:, 4:5]),
                                 inc=("ac", 1))
                    m_rs = S.add("scalar", lambda: nc.scalar.activation(sr6[:, 24:25], sr6[:, 28:29], AF.Exp),
                                 inc=("ac", 1))
                    ob = otok[tg % 2]
                    wa = [("ac", m_rs)]
                    if out_st is not None and tg >= 2:
                        wa.append(("st", out_st))
                    m_a2 = S.add("vector", lambda ob=ob: nc.vector.tensor_scalar_mul(ob[:], atok[:], sr6[:, 24:25]),
                                 waits=wa, inc=("ve", 1), sync=True)
                    prev_apply6 = m_a2
                    out_st = S.add("scalar", lambda ob=ob, tg=tg: nc.scalar.dma_start(
                        out=out[tg * 128:(tg + 1) * 128, :], in_=ob[:]),
                        waits=[("ve", m_a2)], inc=("st", 16))
                ms["p6_st"] = out_st

        if dbg:
            wait_all = [("st", S.planned.get("st", 0)), ("mm", S.planned.get("mm", 0)),
                        ("ve", S.planned.get("ve", 0)), ("ac", S.planned.get("ac", 0)),
                        ("gp", S.planned.get("gp", 0)), ("ld", S.planned.get("ld", 0))]
            srcs = {"qkvT": qkvT, "kvT": kvT, "oT": oT_scr, "tmp": tmp_scr}
            if "p3" in phases:
                srcs["x1b"] = x1b[:]
                srcs["acc"] = acc[:]
            if "p3" in phases and srowlog is not None:
                srcs["srow"] = srowlog[:]
            if "p4" in phases:
                srcs["comb"] = comb[:]
                if p4log is not None:
                    srcs["p4log"] = p4log[:]
            first = True
            for name, dst in dbg_outs.items():
                if name in srcs:
                    S.add("gpsimd", lambda dst=dst, name=name: nc.gpsimd.dma_start(out=dst, in_=srcs[name]),
                          waits=(wait_all if first else []), inc=("gd", 16))
                    first = False
            S.add("gpsimd", lambda: nc.gpsimd.dma_start(out=out[0:1, 0:1], in_=xo[0:1, 0:1]), inc=("gd", 16))
            S.add("gpsimd", lambda: nc.gpsimd.engine_nop(), waits=[("gd", S.planned["gd"])])
        elif "p6" in phases:
            S.add("gpsimd", lambda: nc.gpsimd.engine_nop(),
                  waits=[("st", S.planned.get("st", 0))])
        else:
            S.add("gpsimd", lambda: nc.gpsimd.dma_start(out=out, in_=xo),
                  waits=[("st", S.planned.get("st", 0)), ("mm", S.planned.get("mm", 0))], inc=("gd", 16))
            S.add("gpsimd", lambda: nc.gpsimd.engine_nop(), waits=[("gd", S.planned["gd"])])

        with nc.Block() as block:
            S.emit_all(block, sems)

    return nc


def layout_inputs(inp):
    x = np.asarray(inp["L0_attn_x"], np.float32)
    w_qkv = np.asarray(inp["L0_attn_w_qkv"], np.float32)
    w_proj = np.asarray(inp["L0_attn_w_proj"], np.float32)
    th = np.asarray(inp["L0_attn_theta"], np.float32).reshape(H, E)
    rw = np.asarray(inp["L0_ffn_router_w"], np.float32)
    sg = np.asarray(inp["L0_ffn_shared_gate"], np.float32)
    su = np.asarray(inp["L0_ffn_shared_up"], np.float32)
    sd = np.asarray(inp["L0_ffn_shared_down"], np.float32)
    wg = np.asarray(inp["L0_ffn_w_gate"], np.float32)
    wu = np.asarray(inp["L0_ffn_w_up"], np.float32)
    wd = np.asarray(inp["L0_ffn_w_down"], np.float32)

    wqkv_t = np.ascontiguousarray(
        w_qkv.reshape(48, 128, KK, 128).transpose(0, 3, 2, 1).reshape(48, 128, KK * 128))
    wproj_t = np.ascontiguousarray(
        w_proj.reshape(16, 128, KK, 128).transpose(0, 3, 2, 1).reshape(16, 128, KK * 128))

    def gu_tiles(g, u):
        gt = g.reshape(NII, 128, KK, 128).transpose(0, 3, 2, 1)
        ut = u.reshape(NII, 128, KK, 128).transpose(0, 3, 2, 1)
        s = np.stack([gt, ut], axis=3)
        s = s.reshape(NII, 128, 2, 8, 2, 128).transpose(0, 2, 1, 3, 4, 5)
        return s.reshape(NII, 2, 128, 8 * 2 * 128)
    gu_list = [gu_tiles(sg, su)] + [gu_tiles(wg[e], wu[e]) for e in range(NE)]
    wgu_t = np.ascontiguousarray(np.stack(gu_list, axis=0))

    def dn_tiles(w):
        t = w.reshape(16, 128, NII, 128).transpose(0, 3, 2, 1)
        return t.reshape(16, 128, NII * 128)
    dn_list = [dn_tiles(sd)] + [dn_tiles(wd[e]) for e in range(NE)]
    wdn_t = np.ascontiguousarray(np.stack(dn_list, axis=0))

    wrt_t = np.ascontiguousarray(rw.T.reshape(KK, 128, 8).transpose(1, 0, 2).reshape(128, KK * 8))

    in_maps = []
    for c in range(8):
        b, p = c // 2, c % 2
        x_own = np.ascontiguousarray(x[b, p * 2048:(p + 1) * 2048, :])
        x_oth = np.ascontiguousarray(x[b, (1 - p) * 2048:(2 - p) * 2048, :])
        t0own = np.full((128, 1), float(p * 2048), np.float32)
        t0oth = np.full((128, 1), float((1 - p) * 2048), np.float32)
        mu = np.full((128, 1), float(p), np.float32)
        in_maps.append({
            "xo": x_own, "xh": x_oth, "wqkv": wqkv_t, "wproj": wproj_t,
            "wgu": wgu_t, "wdn": wdn_t, "wrt": wrt_t, "theta": th,
            "t0o": t0own, "t0h": t0oth, "muse": mu,
        })
    return in_maps


def kernel(**inputs):
    in_maps = layout_inputs(inputs)
    nc = build()
    res = run_bass_kernel_spmd(nc, in_maps, core_ids=list(range(8)))
    outs = [np.asarray(res.results[c]["out"]) for c in range(8)]
    full = np.stack(outs, axis=0).reshape(4, 2, 2048, 2048).reshape(4, 4096, 2048)
    return full


# version 14 (final)
# speedup vs baseline: 1.0740x; 1.0740x over previous
import contextlib
import math
import numpy as np
import concourse.bass as bass
import concourse.mybir as mybir
from concourse.bass_utils import run_bass_kernel_spmd

F32 = mybir.dt.float32
F32R = mybir.dt.float32r
BF16 = mybir.dt.bfloat16
I32 = mybir.dt.int32
AF = mybir.ActivationFunctionType
ALU = mybir.AluOpType
AX = mybir.AxisListType

D = 2048
NT = 2048
H = 16
E = 128
CH = 128
NCH = NT // CH
FF = 1408
NII = FF // 128
NE = 8
NPATH = 9
KK = D // 128
TT = NT // 512
PI = math.pi
C1 = 6.28125
C2 = 2 * math.pi - C1
INV2PI = 1.0 / (2 * math.pi)
SLOPES = [2.0 ** (-0.5 * (j + 1)) for j in range(H)]

ENGINES = ("tensor", "vector", "scalar", "gpsimd", "sync")


class Sched:

    def __init__(self):
        self.ops = {e: [] for e in ENGINES}
        self.planned = {}
        self.pending = {e: [] for e in ENGINES}

    def fence(self, extra=()):
        waits = [(s, v) for s, v in self.planned.items()] + list(extra)
        for e in ENGINES:
            self.pending[e] = list(waits)

    def add(self, eng, fn, waits=(), inc=None, sync=False):
        incval = None
        if inc is not None:
            v = self.planned.get(inc[0], 0) + inc[1]
            self.planned[inc[0]] = v
            incval = v
        allw = list(self.pending[eng]) + [(s, v) for (s, v) in waits if v is not None and v > 0]
        self.pending[eng] = []
        self.ops[eng].append((tuple(allw), fn, inc, sync))
        return incval

    def emit_all(self, block, sems):
        for engname in ENGINES:
            oplist = self.ops[engname]
            def make(engname=engname, oplist=oplist):
                def body(eng):
                    lw = {}
                    for waits, fn, inc, opsync in oplist:
                        for s, v in waits:
                            if v > lw.get(s, 0):
                                eng.wait_ge(sems[s], v)
                                lw[s] = v
                        ins = fn()
                        if inc is not None:
                            if isinstance(ins, (list, tuple)):
                                ins = ins[-1]
                            if engname == "scalar" and inc[1] < 16:
                                eng.drain().then_inc(sems[inc[0]], inc[1])
                            elif engname == "gpsimd" and inc[1] < 16:
                                eng.engine_nop().then_inc(sems[inc[0]], inc[1])
                            elif engname == "vector" and opsync and inc[1] < 16:
                                eng.drain().then_inc(sems[inc[0]], inc[1])
                            else:
                                ins.then_inc(sems[inc[0]], inc[1])
                return body
            getattr(block, engname)(make())


def build(phases=("p0", "p1", "p2", "p3", "p4", "p5", "p6"), dbg=()):
    nc = bass.Bass("TRN2", target_bir_lowering=False)
    ap = lambda t: t.ap()

    xo = ap(nc.dram_tensor("xo", [NT, D], F32, kind="ExternalInput"))
    xh = ap(nc.dram_tensor("xh", [NT, D], F32, kind="ExternalInput"))
    wqkv = ap(nc.dram_tensor("wqkv", [48, 128, KK * 128], F32R, kind="ExternalInput"))
    wproj = ap(nc.dram_tensor("wproj", [16, 128, KK * 128], F32R, kind="ExternalInput"))
    wgu = ap(nc.dram_tensor("wgu", [NPATH, NII, 2, 128, 8 * 2 * 128], F32, kind="ExternalInput"))
    wdn = ap(nc.dram_tensor("wdn", [NPATH, 16, 128, NII * 128], F32, kind="ExternalInput"))
    wrt = ap(nc.dram_tensor("wrt", [128, KK * 8], F32, kind="ExternalInput"))
    theta = ap(nc.dram_tensor("theta", [H, E], F32, kind="ExternalInput"))
    t0o = ap(nc.dram_tensor("t0o", [128, 1], F32, kind="ExternalInput"))
    t0h = ap(nc.dram_tensor("t0h", [128, 1], F32, kind="ExternalInput"))
    muse = ap(nc.dram_tensor("muse", [128, 1], F32, kind="ExternalInput"))
    out = ap(nc.dram_tensor("out", [NT, D], F32, kind="ExternalOutput"))

    xT_scr = ap(nc.dram_tensor("xT_scr", [D, NT], F32, kind="Internal"))
    qkvT = ap(nc.dram_tensor("qkvT_scr", [3 * D, NT], F32, kind="Internal"))
    kvT = ap(nc.dram_tensor("kvT_scr", [2 * D, NT], F32, kind="Internal"))
    oT_scr = ap(nc.dram_tensor("oT_scr", [D, NT], F32, kind="Internal"))
    tmp_scr = ap(nc.dram_tensor("tmp_scr", [D, NT], F32, kind="Internal"))

    dbg_outs = {}
    dbg_specs = {"qkvT": ([3 * D, NT], F32), "kvT": ([2 * D, NT], F32),
                 "oT": ([D, NT], F32), "tmp": ([D, NT], F32),
                 "x1b": ([128, KK * NT], BF16), "acc": ([128, KK * NT], BF16),
                 "comb": ([128, NCH * 8], F32), "srow": ([128, NCH * 8], F32), "p4log": ([128, 8 * 32], F32)}
    for name in dbg:
        if name not in dbg_specs:
            continue
        shp, dt = dbg_specs[name]
        dbg_outs[name] = ap(nc.dram_tensor("dbg_" + name, shp, dt, kind="ExternalOutput"))

    S = Sched()
    es = contextlib.ExitStack()
    with es:
        sems = {s: es.enter_context(nc.semaphore(s))
                for s in ["ld", "st", "mm", "ve", "ac", "gp", "gd"]}
        sb = lambda n, shp, dt: es.enter_context(nc.sbuf_tensor(n, shp, dt))

        consts = sb("consts", [128, 12], F32)
        idf = sb("idf", [128, 128], F32)
        idb = sb("idb", [128, 128], BF16)
        ones1 = sb("ones1", [1, 128], F32R)
        iobuf = sb("iobuf", [128, 128], F32)
        thetaT = sb("thetaT", [128, H], F32)
        theta_st = sb("theta_st", [16, 128], F32)
        musec = sb("musec", [128, 1], F32)
        t0oc = sb("t0oc", [128, 1], F32)
        t0hc = sb("t0hc", [128, 1], F32)

        pre_ps = nc.psum_tensor("pre_ps", [128, 512], F32)
        with pre_ps as pre_ps:
            m_th = S.add("sync", lambda: nc.sync.dma_start(out=theta_st[:], in_=theta), inc=("ld", 16))
            S.add("sync", lambda: nc.sync.dma_start(out=t0oc[:], in_=t0o), inc=("ld", 16))
            S.add("sync", lambda: nc.sync.dma_start(out=t0hc[:], in_=t0h), inc=("ld", 16))
            m_aux = S.add("sync", lambda: nc.sync.dma_start(out=musec[:], in_=muse), inc=("ld", 16))
            m_io = S.add("gpsimd", lambda: nc.gpsimd.iota(iobuf[:], pattern=[[1, 128]], base=0,
                                                          channel_multiplier=-1,
                                                          allow_small_or_imprecise_dtypes=True), inc=("gp", 1))
            m_idf = S.add("vector", lambda: nc.vector.tensor_single_scalar(idf[:], iobuf[:], 0.0, ALU.is_equal),
                          waits=[("gp", m_io)], inc=("ve", 1))
            S.add("vector", lambda: nc.vector.tensor_copy(idb[:], idf[:]), inc=("ve", 1))
            S.add("vector", lambda: nc.vector.memset(ones1[:].bitcast(F32), 1.0), inc=("ve", 1))
            S.add("vector", lambda: nc.vector.memset(consts[:, 0:1], -1.0), inc=("ve", 1))
            S.add("vector", lambda: nc.vector.memset(consts[:, 1:2], 1e-5), inc=("ve", 1))
            S.add("vector", lambda: nc.vector.memset(consts[:, 2:3], 1.0), inc=("ve", 1))
            S.add("vector", lambda: nc.vector.memset(consts[:, 3:4], 0.0), inc=("ve", 1))
            S.add("vector", lambda: nc.vector.memset(consts[:, 4:5], -0.5), inc=("ve", 1))
            S.add("vector", lambda: nc.vector.memset(consts[:, 5:6], INV2PI), inc=("ve", 1))
            S.add("vector", lambda: nc.vector.memset(consts[:, 6:7], -PI), inc=("ve", 1))
            S.add("vector", lambda: nc.vector.memset(consts[:, 7:8], 2 * PI), inc=("ve", 1))
            S.add("vector", lambda: nc.vector.memset(consts[:, 8:9], PI / 2), inc=("ve", 1))
            S.add("vector", lambda: nc.vector.memset(consts[:, 9:10], PI), inc=("ve", 1))
            m_cst = S.add("vector", lambda: nc.vector.memset(consts[:, 11:12], 1.0 / D), inc=("ve", 1))
            for fn in (AF.Exp, AF.Sin, AF.Square, AF.Ln, AF.Sigmoid):
                S.add("scalar", lambda fn=fn: nc.scalar.activation(consts[:, 10:11], consts[:, 2:3], fn),
                      waits=[("ve", m_cst)], inc=("ac", 1))
            m_warm = S.add("scalar", lambda: nc.scalar.activation(consts[:, 10:11], consts[:, 2:3], AF.Exp),
                           inc=("ac", 1))
            ms_warm = m_warm
            m_tht = S.add("tensor", lambda: nc.tensor.transpose(pre_ps[:, 0:16], theta_st[:], idf[0:16, 0:16]),
                          waits=[("ld", m_th), ("ve", m_idf)], inc=("mm", 1))
            m_thT = S.add("vector", lambda: nc.vector.tensor_copy(thetaT[:], pre_ps[:, 0:16]),
                          waits=[("mm", m_tht)], inc=("ve", 1))

        ms = {"pre_ve": m_thT, "pre_gp": m_io, "aux_ld": m_aux}

        last_p1_mm = None
        if "p0" in phases:
            p01 = contextlib.ExitStack()
            with p01:
                xT = p01.enter_context(nc.sbuf_tensor("xT", [128, KK * NT], F32R))
                xld = [p01.enter_context(nc.sbuf_tensor(f"xld{i}", [128, D], F32)) for i in range(2)]
                wqb = [p01.enter_context(nc.sbuf_tensor(f"wqb{i}", [128, KK * 128], F32R)) for i in range(2)]
                qrow = [p01.enter_context(nc.sbuf_tensor(f"qrow{i}", [128, NT], F32)) for i in range(2)]
                ps = [p01.enter_context(nc.psum_tensor(f"p1ps{i}", [128, 512], F32)) for i in range(8)]
                bank_free = [ms["pre_ve"]] * 8

                def build_xT(xsrc, first_waits):
                    ld_ms = [None] * NCH
                    tg_done = [None] * NCH
                    for tg in range(NCH):
                        buf = xld[tg % 2]
                        w = list(first_waits) if tg == 0 else []
                        if tg >= 2:
                            w.append(("ve", tg_done[tg - 2]))
                        ld_ms[tg] = S.add("sync", lambda buf=buf, tg=tg, xsrc=xsrc:
                                          nc.sync.dma_start(out=buf[:], in_=xsrc[tg * 128:(tg + 1) * 128, :]),
                                          waits=w, inc=("ld", 16))
                        for g4 in range(4):
                            b = (tg % 2) * 4 + g4
                            bank = ps[b]
                            tw = [("ld", ld_ms[tg]), ("ve", bank_free[b])]
                            mlast = None
                            for j in range(4):
                                kkc = g4 * 4 + j
                                mlast = S.add("tensor", lambda bank=bank, j=j, kkc=kkc, buf=buf:
                                              nc.tensor.transpose(bank[:, j * 128:(j + 1) * 128],
                                                                  buf[:, kkc * 128:(kkc + 1) * 128], idf[:]),
                                              waits=(tw if j == 0 else []), inc=("mm", 1))

                            def ev(bank=bank, g4=g4, tg=tg):
                                ins = None
                                for j in range(4):
                                    kkc = g4 * 4 + j
                                    ins = nc.vector.tensor_copy(
                                        xT[:, kkc * NT + tg * 128: kkc * NT + (tg + 1) * 128],
                                        bank[:, j * 128:(j + 1) * 128])
                                return ins
                            bank_free[b] = S.add("vector", ev, waits=[("mm", mlast)], inc=("ve", 1))
                        tg_done[tg] = bank_free[(tg % 2) * 4 + 3]
                    return tg_done[NCH - 1], max(filter(None, [bank_free[i] for i in range(8)]))

                def spill_xT(last_ve):
                    mlast = None
                    for kkc in range(KK):
                        mlast = S.add("scalar", lambda kkc=kkc:
                                      nc.scalar.dma_start(out=xT_scr[kkc * 128:(kkc + 1) * 128, :],
                                                          in_=xT[:, kkc * NT:(kkc + 1) * NT].bitcast(F32)),
                                      waits=([("ve", last_ve)] if kkc == 0 else []), inc=("st", 16))
                    return mlast

                def qkv_pass(ob_list, dst, xT_ready_ve, start_idx_rows):
                    n = len(ob_list)
                    wld = [None] * n
                    blk_mm = [None] * n
                    evac_ve = [None] * n
                    spill_st = [None] * n
                    for i, ob in enumerate(ob_list):
                        wbuf = wqb[i % 2]
                        w = []
                        if i >= 2:
                            w.append(("mm", blk_mm[i - 2]))
                        wld[i] = S.add("sync", lambda wbuf=wbuf, ob=ob:
                                       nc.sync.dma_start(out=wbuf[:], in_=wqkv[ob]),
                                       waits=w, inc=("ld", 16))
                        mlast = None
                        for kk in range(KK):
                            for tt in range(TT):
                                b = (i % 2) * 4 + tt
                                tw = []
                                if kk == 0:
                                    tw = [("ld", wld[i]), ("ve", bank_free[b])]
                                    if tt == 0:
                                        tw.append(("ve", xT_ready_ve))
                                mlast = S.add("tensor", lambda b=b, kk=kk, tt=tt, wbuf=wbuf:
                                              nc.tensor.matmul(ps[b][:, 0:512],
                                                               wbuf[:, kk * 128:(kk + 1) * 128],
                                                               xT[:, kk * NT + tt * 512: kk * NT + (tt + 1) * 512],
                                                               start=(kk == 0), stop=(kk == KK - 1)),
                                              waits=tw, inc=("mm", 1) if (kk == KK - 1 and tt == TT - 1) else None)
                        blk_mm[i] = S.planned["mm"]

                        qb = qrow[i % 2]

                        def ev(i=i, qb=qb):
                            ins = None
                            for tt in range(TT):
                                b = (i % 2) * 4 + tt
                                ins = nc.vector.tensor_copy(qb[:, tt * 512:(tt + 1) * 512], ps[b][:, 0:512])
                            return ins
                        wv = [("mm", blk_mm[i])]
                        if i >= 2:
                            wv.append(("st", spill_st[i - 2]))
                        evac_ve[i] = S.add("vector", ev, waits=wv, inc=("ve", 1))
                        for b in range((i % 2) * 4, (i % 2) * 4 + 4):
                            bank_free[b] = evac_ve[i]
                        r0 = (start_idx_rows + i) * 128
                        spill_st[i] = S.add("scalar", lambda qb=qb, r0=r0, dst=dst:
                                            nc.scalar.dma_start(out=dst[r0:r0 + 128, :], in_=qb[:]),
                                            waits=[("ve", evac_ve[i])], inc=("st", 16))
                    return blk_mm[n - 1], spill_st[n - 1], max(spill_st)

                own_last_ve, _ = build_xT(xo, [])
                m_spill_xT = spill_xT(own_last_ve)
                p1a_mm, _, p1a_st = qkv_pass(list(range(48)), qkvT, own_last_ve, 0)
                oth_last_ve, _ = build_xT(xh, [("mm", p1a_mm)])
                p1b_mm, _, p1b_st = qkv_pass(list(range(16, 48)), kvT, oth_last_ve, 0)
                last_p1_mm = p1b_mm
                ms["p1_st"] = max(p1a_st, p1b_st, m_spill_xT)
                ms["p1_mm"] = p1b_mm


        if "p2" in phases:
            p2 = contextlib.ExitStack()
            with p2:
                sb2 = lambda n, shp, dt: p2.enter_context(nc.sbuf_tensor(n, shp, dt))
                qh = sb2("qh", [128, NT], F32)
                kh = sb2("kh", [128, NT], F32)
                vh = sb2("vh", [128, NT], F32)
                kho = sb2("kho", [128, NT], F32)
                vho = sb2("vho", [128, NT], F32)
                Ctab = sb2("Ctab", [128, 2 * NT], F32)
                Stab = sb2("Stab", [128, 2 * NT], F32)
                WG = 1024
                thb = sb2("thb", [128, WG], F32)
                t1b = sb2("t1b", [128, WG], F32)
                t2b = sb2("t2b", [128, WG], F32)
                POSo = sb2("POSo", [128, 2 * NT], F32)
                decayT = sb2("decayT", [128, H * 128], BF16)
                kdecc = sb2("kdecc", [128, H], F32)
                dfc = sb2("dfc", [128, 128], F32)
                c127 = sb2("c127", [128, 1], F32)
                scol = [sb2(f"scol{p}", [128, 1], F32) for p in range(2)]
                QD = [sb2(f"QD{p}", [128, 512], F32) for p in range(2)]
                QC = [sb2(f"QC{p}", [128, NT], BF16) for p in range(2)]
                QS = [sb2(f"QS{p}", [128, NT], BF16) for p in range(2)]
                QCD = [sb2(f"QCD{p}", [128, NT], BF16) for p in range(2)]
                QSD = [sb2(f"QSD{p}", [128, NT], BF16) for p in range(2)]
                KCt = [sb2(f"KCt{p}", [128, NT], BF16) for p in range(2)]
                KSt = [sb2(f"KSt{p}", [128, NT], BF16) for p in range(2)]
                vtok = [sb2(f"vtok{p}", [128, NT], BF16) for p in range(2)]
                vdec = [sb2(f"vdec{p}", [128, NT], BF16) for p in range(2)]
                vdeco = [sb2(f"vdeco{p}", [128, NT], BF16) for p in range(2)]
                o_h = [sb2(f"o_h{p}", [128, NT], F32) for p in range(2)]
                stCS = [sb2(f"stCS{p}", [128, 256], F32) for p in range(2)]
                stCSb = [sb2(f"stCSb{p}", [128, 256], BF16) for p in range(2)]
                ScSs = [sb2(f"ScSs{p}", [128, 256], BF16) for p in range(2)]
                kcs = [sb2(f"kcs{p}", [128, 256], BF16) for p in range(2)]
                scTb = [sb2(f"scTb{p}", [128, 128], BF16) for p in range(2)]

                ps_sc = p2.enter_context(nc.psum_tensor("ps_sc", [128, 512], F32))
                ps_o = p2.enter_context(nc.psum_tensor("ps_o", [128, 512], F32))
                ps_st = p2.enter_context(nc.psum_tensor("ps_st", [128, 512], F32))
                ps_stB = p2.enter_context(nc.psum_tensor("ps_stB", [128, 512], F32))
                pbf = p2.enter_context(nc.psum_tensor("pbf", [128, 1024], BF16))
                ps_c = [p2.enter_context(nc.psum_tensor(f"ps_c{p}", [128, 512], F32)) for p in range(2)]

                S.fence()

                m_pos0 = S.add("gpsimd", lambda: nc.gpsimd.iota(POSo[:, 0:NT], pattern=[[1, NT]], base=0,
                                                                channel_multiplier=0,
                                                                allow_small_or_imprecise_dtypes=True), inc=("gp", 1))
                m_posb = S.add("gpsimd", lambda: nc.gpsimd.tensor_tensor(
                    POSo[:, NT:2 * NT], POSo[:, 0:NT], t0hc[:, 0:1].to_broadcast([128, NT]), ALU.add),
                               waits=[("ld", ms["aux_ld"]), ("ve", ms["pre_ve"])], inc=("gp", 1))
                m_pos = S.add("gpsimd", lambda: nc.gpsimd.tensor_tensor(
                    POSo[:, 0:NT], POSo[:, 0:NT], t0oc[:, 0:1].to_broadcast([128, NT]), ALU.add),
                              inc=("gp", 1))
                m_dfc = S.add("vector", lambda: nc.vector.tensor_scalar_max(dfc[:], iobuf[:], 0.0), inc=("ve", 1))
                m_c127 = S.add("gpsimd", lambda: nc.gpsimd.iota(c127[:], pattern=[[0, 1]], base=127,
                                                                channel_multiplier=-1,
                                                                allow_small_or_imprecise_dtypes=True), inc=("gp", 1))
                m_arg = None
                for h in range(H):
                    s = SLOPES[h]
                    m_arg = S.add("vector", lambda h=h, s=s:
                                  nc.vector.tensor_scalar_mul(decayT[:, h * 128:(h + 1) * 128], dfc[:], -s),
                                  waits=[("ve", m_dfc)], inc=("ve", 1))
                    m_arg = S.add("vector", lambda h=h, s=s:
                                  nc.vector.tensor_scalar_mul(kdecc[:, h:h + 1], c127[:], -s),
                                  waits=[("gp", m_c127)], inc=("ve", 1))
                m_exp1 = S.add("scalar", lambda: nc.scalar.activation(decayT[:], decayT[:], AF.Exp),
                               waits=[("ve", m_arg)], inc=("ac", 1))
                m_qdi = S.add("gpsimd", lambda: nc.gpsimd.iota(dfc[:], pattern=[[1, 128]], base=1,
                                                               channel_multiplier=0,
                                                               allow_small_or_imprecise_dtypes=True),
                              waits=[("ve", m_arg)], inc=("gp", 1))
                m_exp2 = S.add("scalar", lambda: nc.scalar.activation(kdecc[:], kdecc[:], AF.Exp),
                               inc=("ac", 1))
                m_mask = None
                for h in range(H):
                    m_mask = S.add("gpsimd", lambda h=h:
                                   nc.gpsimd.affine_select(decayT[:, h * 128:(h + 1) * 128],
                                                           decayT[:, h * 128:(h + 1) * 128],
                                                           pattern=[[1, 128]], base=0, channel_multiplier=-1,
                                                           compare_op=ALU.is_ge, fill=0.0),
                                   waits=[("ac", m_exp1)], inc=("gp", 1))
                ms["p2_pre_gp"] = m_mask
                ms["p2_pre_ac"] = m_exp2

                genstate = {"v8": None, "acos": None}

                def gen_pass(h, gi):
                    thc = thetaT[:, h:h + 1]
                    off = gi * WG
                    w0 = [("gp", m_pos), ("ve", ms["pre_ve"])]
                    if genstate["v8"] is not None:
                        w0.append(("ve", genstate["v8"]))
                    if genstate["acos"] is not None:
                        w0.append(("ac", genstate["acos"]))
                    gth = S.add("gpsimd", lambda thc=thc, off=off: nc.gpsimd.tensor_tensor(
                        thb[:], POSo[:, off:off + WG], thc.to_broadcast([128, WG]), ALU.mult),
                                waits=w0, inc=("gp", 1))
                    g1 = S.add("gpsimd", lambda: nc.gpsimd.tensor_tensor(
                        t1b[:], thb[:], consts[:, 5:6].to_broadcast([128, WG]), ALU.mult),
                               inc=("gp", 1))
                    v1 = S.add("vector", lambda: nc.vector.tensor_copy(t2b[:].bitcast(I32), t1b[:]),
                               waits=[("gp", g1)], inc=("ve", 1))
                    v2 = S.add("vector", lambda: nc.vector.tensor_copy(t1b[:], t2b[:].bitcast(I32)),
                               inc=("ve", 1))
                    v3 = S.add("vector", lambda: nc.vector.scalar_tensor_tensor(
                        t2b[:], t1b[:], -2 * PI, thb[:], ALU.mult, ALU.add), inc=("ve", 1))
                    v4 = S.add("vector", lambda: nc.vector.tensor_single_scalar(
                        t1b[:], t2b[:], PI, ALU.is_gt), inc=("ve", 1))
                    v5 = S.add("vector", lambda: nc.vector.scalar_tensor_tensor(
                        t2b[:], t1b[:], -2 * PI, t2b[:], ALU.mult, ALU.add), inc=("ve", 1))
                    a_sin = S.add("scalar", lambda off=off: nc.scalar.activation(
                        Stab[:, off:off + WG], t2b[:], AF.Sin, scale=consts[:, 2:3]),
                                  waits=[("ve", v5)], inc=("ac", 1))
                    v6 = S.add("vector", lambda: nc.vector.tensor_scalar(
                        t1b[:], t2b[:], -1.0, PI / 2, ALU.mult, ALU.add), inc=("ve", 1))
                    v7 = S.add("vector", lambda: nc.vector.tensor_single_scalar(
                        thb[:], t1b[:], PI, ALU.is_gt), inc=("ve", 1))
                    v8 = S.add("vector", lambda: nc.vector.scalar_tensor_tensor(
                        t1b[:], thb[:], -2 * PI, t1b[:], ALU.mult, ALU.add), inc=("ve", 1))
                    a_cos = S.add("scalar", lambda off=off: nc.scalar.activation(
                        Ctab[:, off:off + WG], t1b[:], AF.Sin, scale=consts[:, 2:3]),
                                  waits=[("ve", v8)], inc=("ac", 1))
                    genstate["v8"] = v8
                    genstate["acos"] = a_cos
                    return a_cos

                hstate = [dict() for _ in range(H)]
                pbf_free = {}

                def head_pipeline(h):
                    P = h % 2
                    s = SLOPES[h]
                    bd = math.exp(-s * CH)
                    dT = decayT[:, h * 128:(h + 1) * 128]
                    h1 = hstate[h - 1] if h >= 1 else None
                    h2 = hstate[h - 2] if h >= 2 else None
                    mm_war = h2["mm_done"] if h >= 2 else ms["p1_mm"]

                    if h == 0:
                        wld_ = [("st", ms["p1_st"])]
                    else:
                        while "ld_ok" not in h1:
                            yield
                        wld_ = [("ve", h1["m_k"]), ("ac", h1["vev2"])]
                    ld = {}
                    for nm, buf, scr, r0 in (("qh", qh, qkvT, h * 128), ("kh", kh, qkvT, D + h * 128),
                                             ("vh", vh, qkvT, 2 * D + h * 128),
                                             ("kho", kho, kvT, h * 128), ("vho", vho, kvT, D + h * 128)):
                        ld[nm] = S.add("sync", lambda buf=buf, scr=scr, r0=r0:
                                       nc.sync.dma_start(out=buf[:], in_=scr[r0:r0 + 128, :]),
                                       waits=wld_, inc=("ld", 16))
                        wld_ = []
                    yield

                    m_z = S.add("vector", lambda P=P: nc.vector.memset(stCS[P][:], 0.0),
                                waits=[("mm", mm_war)], inc=("ve", 1))
                    wzb = [("ac", h2["st_cast15"])] if h >= 2 else []
                    m_zb = S.add("vector", lambda P=P: nc.vector.memset(stCSb[P][:], 0.0),
                                 waits=wzb, inc=("ve", 1))
                    wsc_ = [("ac", h2["qd_exp"])] if h >= 2 else []
                    m_sc = S.add("vector", lambda P=P, s=s: nc.vector.memset(scol[P][:], -s),
                                 waits=wsc_, inc=("ve", 1), sync=True)
                    wqd = [("gp", m_qdi), ("ve", m_sc)]
                    if h >= 2:
                        wqd.append(("ve", h2["m_p2"]))
                    a_qd = S.add("scalar", lambda P=P: nc.scalar.activation(
                        QD[P][:, 0:128], dfc[:], AF.Exp, scale=scol[P][:, 0:1]),
                                 waits=wqd, inc=("ac", 1))
                    hstate[h]["qd_exp"] = a_qd
                    yield

                    a_oth = None
                    for q in range(2):
                        a_oth = gen_pass(h, NT // WG + q)
                        yield
                    wprep = [("ac", a_oth), ("ld", ld["kho"])]
                    m_kco = S.add("vector", lambda P=P: nc.vector.tensor_mul(
                        KCt[P][:], kho[:], Ctab[:, NT:2 * NT]), waits=wprep, inc=("ve", 1))
                    m_kso = S.add("vector", lambda P=P: nc.vector.tensor_mul(
                        KSt[P][:], kho[:], Stab[:, NT:2 * NT]), inc=("ve", 1), sync=True)
                    yield
                    a_own = None
                    for q in range(2):
                        a_own = gen_pass(h, q)
                        yield

                    vevacs = []
                    for g4 in range(4):
                        wv = [("ld", ld["vho"])]
                        if g4 == 0 and h >= 2:
                            wv.append(("ve", h2["corr_ve"]))
                        if vevacs:
                            wv.append(("ac", vevacs[-1]))
                        ml = None
                        for j in range(4):
                            c = g4 * 4 + j
                            ml = S.add("tensor", lambda j=j, c=c, P=P: nc.tensor.transpose(
                                ps_c[P][:, j * 128:(j + 1) * 128], vho[:, c * 128:(c + 1) * 128], idf[:]),
                                       waits=(wv if j == 0 else []), inc=("mm", 1))

                        def vev(g4=g4, P=P):
                            ins = None
                            for j in range(4):
                                c = g4 * 4 + j
                                ins = nc.scalar.copy(vdeco[P][:, c * 128:(c + 1) * 128],
                                                     ps_c[P][:, j * 128:(j + 1) * 128])
                            return ins
                        vevacs.append(S.add("scalar", vev, waits=[("mm", ml)], inc=("ac", 1)))
                        yield
                    m_vd = S.add("vector", lambda P=P, h=h: nc.vector.tensor_scalar_mul(
                        vdeco[P][:], vdeco[P][:], kdecc[:, h:h + 1]),
                                 waits=[("ac", vevacs[-1]), ("ac", ms["p2_pre_ac"])], inc=("ve", 1), sync=True)
                    yield

                    st_up = m_z
                    st_hist = [None] * NCH
                    for c in range(NCH):
                        sbk = ps_st if c % 2 == 0 else ps_stB
                        slot = P * 4 + (c % 2) * 2
                        wv = [("ve", m_kso)]
                        if slot in pbf_free:
                            wv.append(pbf_free[slot])
                        t1 = S.add("tensor", lambda slot=slot, c=c, P=P: nc.tensor.transpose(
                            pbf[:, slot * 128:(slot + 1) * 128], KCt[P][:, c * 128:(c + 1) * 128], idb[:]),
                                   waits=wv, inc=("mm", 1))
                        t2m = S.add("tensor", lambda slot=slot, c=c, P=P: nc.tensor.transpose(
                            pbf[:, (slot + 1) * 128:(slot + 2) * 128], KSt[P][:, c * 128:(c + 1) * 128], idb[:]),
                                   inc=("mm", 1))
                        kT_ac = S.add("scalar", lambda slot=slot, P=P: nc.scalar.copy(
                            kcs[P][:], pbf[:, slot * 128:(slot + 2) * 128]),
                                      waits=[("mm", t2m)], inc=("ac", 1))
                        ws1 = [("ac", kT_ac), ("ve", m_vd)]
                        if c >= 2:
                            ws1.append(("ve", st_hist[c - 2]))
                        s1 = S.add("tensor", lambda c=c, P=P, sbk=sbk: nc.tensor.matmul(
                            sbk[:, P * 256:P * 256 + 128], kcs[P][:, 0:128], vdeco[P][:, c * 128:(c + 1) * 128],
                            start=True, stop=True),
                                   waits=ws1, inc=("mm", 1))
                        s2 = S.add("tensor", lambda c=c, P=P, sbk=sbk: nc.tensor.matmul(
                            sbk[:, P * 256 + 128:P * 256 + 256], kcs[P][:, 128:256],
                            vdeco[P][:, c * 128:(c + 1) * 128],
                            start=True, stop=True), inc=("mm", 1))
                        st_up = S.add("vector", lambda bd=bd, P=P, sbk=sbk: nc.vector.scalar_tensor_tensor(
                            stCS[P][:], stCS[P][:], bd, sbk[:, P * 256:P * 256 + 256], ALU.mult, ALU.add),
                                      waits=[("mm", s2)], inc=("ve", 1), sync=True)
                        st_hist[c] = st_up
                        pbf_free[slot] = ("ac", kT_ac)
                        yield
                    m_sv = S.add("vector", lambda P=P: nc.vector.tensor_scalar_mul(
                        ScSs[P][:], stCS[P][:], musec[:, 0:1]), waits=[("ve", st_up)], inc=("ve", 1), sync=True)
                    m_z2 = S.add("vector", lambda P=P: nc.vector.memset(stCS[P][:], 0.0), inc=("ve", 1))
                    yield

                    wq = [("ac", a_own), ("ld", ld["qh"]), ("ld", ld["kh"]), ("ve", m_sv)]
                    m_p = S.add("vector", lambda P=P: nc.vector.tensor_mul(QC[P][:], qh[:], Ctab[:, 0:NT]),
                                waits=wq, inc=("ve", 1))
                    m_p = S.add("vector", lambda P=P: nc.vector.tensor_mul(QS[P][:], qh[:], Stab[:, 0:NT]),
                                inc=("ve", 1))
                    cq = S.add("vector", lambda P=P: nc.vector.tensor_copy(QD[P][:, 128:256], QD[P][:, 0:128]),
                               waits=[("ac", a_qd)], inc=("ve", 1))
                    cq = S.add("vector", lambda P=P: nc.vector.tensor_copy(QD[P][:, 256:384], QD[P][:, 0:128]),
                               inc=("ve", 1))
                    cq = S.add("vector", lambda P=P: nc.vector.tensor_copy(QD[P][:, 384:512], QD[P][:, 0:128]),
                               inc=("ve", 1), sync=True)
                    m_p2 = None
                    for t in range(TT):
                        m_p2 = S.add("vector", lambda t=t, P=P: nc.vector.tensor_mul(
                            QCD[P][:, t * 512:(t + 1) * 512], QC[P][:, t * 512:(t + 1) * 512], QD[P][:]),
                                     inc=("ve", 1))
                        m_p2 = S.add("vector", lambda t=t, P=P: nc.vector.tensor_mul(
                            QSD[P][:, t * 512:(t + 1) * 512], QS[P][:, t * 512:(t + 1) * 512], QD[P][:]),
                                     inc=("ve", 1), sync=(t == TT - 1))
                    if "nointer" in dbg:
                        m_p2 = S.add("vector", lambda P=P: nc.vector.memset(QCD[P][:], 0.0), inc=("ve", 1))
                        m_p2 = S.add("vector", lambda P=P: nc.vector.memset(QSD[P][:], 0.0), inc=("ve", 1), sync=True)
                    hstate[h]["m_p2"] = m_p2
                    m_k = S.add("vector", lambda P=P: nc.vector.tensor_mul(KCt[P][:], kh[:], Ctab[:, 0:NT]),
                                inc=("ve", 1))
                    m_k = S.add("vector", lambda P=P: nc.vector.tensor_mul(KSt[P][:], kh[:], Stab[:, 0:NT]),
                                inc=("ve", 1), sync=True)
                    yield

                    vevacs2 = []
                    for g4 in range(4):
                        wv = [("ld", ld["vh"]), ("ac", (vevacs2[-1] if vevacs2 else vevacs[-1]))]
                        ml = None
                        for j in range(4):
                            c = g4 * 4 + j
                            ml = S.add("tensor", lambda j=j, c=c, P=P: nc.tensor.transpose(
                                ps_c[P][:, j * 128:(j + 1) * 128], vh[:, c * 128:(c + 1) * 128], idf[:]),
                                       waits=(wv if j == 0 else []), inc=("mm", 1))

                        def vev2(g4=g4, P=P):
                            ins = None
                            for j in range(4):
                                c = g4 * 4 + j
                                ins = nc.scalar.copy(vtok[P][:, c * 128:(c + 1) * 128],
                                                     ps_c[P][:, j * 128:(j + 1) * 128])
                            return ins
                        vevacs2.append(S.add("scalar", vev2, waits=[("mm", ml)], inc=("ac", 1)))
                        yield
                    m_vd2 = S.add("vector", lambda P=P, h=h: nc.vector.tensor_scalar_mul(
                        vdec[P][:], vtok[P][:], kdecc[:, h:h + 1]),
                                  waits=[("ac", vevacs2[-1])], inc=("ve", 1), sync=True)
                    hstate[h]["m_k"] = m_k
                    hstate[h]["vev2"] = vevacs2[-1]
                    hstate[h]["ld_ok"] = True
                    yield

                    sc_hist = [None] * NCH
                    ev_o = [None] * NCH
                    st_hist2 = [None] * NCH
                    s2_hist = [None] * NCH
                    st_up2 = m_zb
                    st_cast = None
                    for c in range(NCH):
                        cp = c % 2
                        sbk = ps_st if cp == 0 else ps_stB
                        sco = P * 128 + cp * 256
                        wsc = [("ve", m_k), ("ve", m_p2)]
                        if c >= 2:
                            wsc.append(("ve", sc_hist[c - 2]))
                        m1 = S.add("tensor", lambda c=c, P=P, sco=sco: nc.tensor.matmul(
                            ps_sc[:, sco:sco + 128], KCt[P][:, c * 128:(c + 1) * 128],
                            QC[P][:, c * 128:(c + 1) * 128], start=True, stop=False),
                                   waits=wsc, inc=("mm", 1))
                        m2 = S.add("tensor", lambda c=c, P=P, sco=sco: nc.tensor.matmul(
                            ps_sc[:, sco:sco + 128], KSt[P][:, c * 128:(c + 1) * 128],
                            QS[P][:, c * 128:(c + 1) * 128], start=False, stop=True), inc=("mm", 1))
                        sc_free = S.add("vector", lambda dT=dT, P=P, sco=sco: nc.vector.tensor_mul(
                            scTb[P][:], ps_sc[:, sco:sco + 128], dT),
                                        waits=[("mm", m2), ("gp", ms["p2_pre_gp"])], inc=("ve", 1), sync=True)
                        sc_hist[c] = sc_free
                        slot = P * 4 + cp * 2
                        wv = []
                        if slot in pbf_free:
                            wv.append(pbf_free[slot])
                        t1 = S.add("tensor", lambda slot=slot, c=c, P=P: nc.tensor.transpose(
                            pbf[:, slot * 128:(slot + 1) * 128], KCt[P][:, c * 128:(c + 1) * 128], idb[:]),
                                   waits=wv, inc=("mm", 1))
                        t2m = S.add("tensor", lambda slot=slot, c=c, P=P: nc.tensor.transpose(
                            pbf[:, (slot + 1) * 128:(slot + 2) * 128], KSt[P][:, c * 128:(c + 1) * 128], idb[:]),
                                   inc=("mm", 1))
                        kT_ac = S.add("scalar", lambda slot=slot, P=P: nc.scalar.copy(
                            kcs[P][:], pbf[:, slot * 128:(slot + 2) * 128]),
                                      waits=[("mm", t2m)], inc=("ac", 1))
                        pbf_free[slot] = ("ac", kT_ac)
                        ws1 = [("ac", kT_ac), ("ve", m_vd2)]
                        if c >= 2:
                            ws1.append(("ve", st_hist2[c - 2]))
                        s1 = S.add("tensor", lambda c=c, P=P, sbk=sbk: nc.tensor.matmul(
                            sbk[:, P * 256:P * 256 + 128], kcs[P][:, 0:128], vdec[P][:, c * 128:(c + 1) * 128],
                            start=True, stop=True),
                                   waits=ws1, inc=("mm", 1))
                        s2 = S.add("tensor", lambda c=c, P=P, sbk=sbk: nc.tensor.matmul(
                            sbk[:, P * 256 + 128:P * 256 + 256], kcs[P][:, 128:256],
                            vdec[P][:, c * 128:(c + 1) * 128],
                            start=True, stop=True), inc=("mm", 1))
                        def tail(cc, sbk2):
                            nonlocal st_up2, st_cast
                            st_up2 = S.add("vector", lambda bd=bd, P=P, sbk2=sbk2:
                                           nc.vector.scalar_tensor_tensor(
                                               stCS[P][:], stCS[P][:], bd, sbk2[:, P * 256:P * 256 + 256],
                                               ALU.mult, ALU.add),
                                           waits=[("mm", s2_hist[cc])], inc=("ve", 1), sync=True)
                            st_hist2[cc] = st_up2
                            st_cast = S.add("scalar", lambda P=P: nc.scalar.copy(stCSb[P][:], stCS[P][:]),
                                            waits=[("ve", st_up2)], inc=("ac", 1))
                        if c > 0:
                            tail(c - 1, ps_st if (c - 1) % 2 == 0 else ps_stB)
                        wo = [("ve", sc_free), ("ac", st_cast if c > 0 else None)]
                        wo = [(a, b) for a, b in wo if b is not None]
                        if c >= 2:
                            wo.append(("ac", ev_o[c - 2]))
                        else:
                            wo.append(("ac", vevacs2[-1]))
                        o1 = S.add("tensor", lambda c=c, P=P, sco=sco: nc.tensor.matmul(
                            ps_o[:, sco:sco + 128], vtok[P][:, c * 128:(c + 1) * 128], scTb[P][:],
                            start=True, stop=False), waits=wo, inc=("mm", 1))
                        o2 = S.add("tensor", lambda c=c, P=P, sco=sco: nc.tensor.matmul(
                            ps_o[:, sco:sco + 128], stCSb[P][:, 0:128], QCD[P][:, c * 128:(c + 1) * 128],
                            start=False, stop=False), inc=("mm", 1))
                        o3 = S.add("tensor", lambda c=c, P=P, sco=sco: nc.tensor.matmul(
                            ps_o[:, sco:sco + 128], stCSb[P][:, 128:256], QSD[P][:, c * 128:(c + 1) * 128],
                            start=False, stop=True), inc=("mm", 1))
                        s2_hist[c] = s2
                        wev = [("mm", o3)]
                        if c == 0 and h >= 2:
                            wev.append(("st", h2["spill"]))
                        ev_o[c] = S.add("scalar", lambda c=c, P=P, sco=sco: nc.scalar.copy(
                            o_h[P][:, c * 128:(c + 1) * 128], ps_o[:, sco:sco + 128]),
                                        waits=wev, inc=("ac", 1))
                        yield
                    tail(NCH - 1, ps_st if (NCH - 1) % 2 == 0 else ps_stB)
                    hstate[h]["st_cast15"] = st_cast

                    corr_ve = None
                    for t in range(TT):
                        wcc = [("ve", m_sv), ("ac", ev_o[4 * t + 3]), ("ac", vevacs2[-1])]
                        if corr_ve is not None:
                            wcc.append(("ve", corr_ve))
                        c1m = S.add("tensor", lambda t=t, P=P: nc.tensor.matmul(
                            ps_c[P][:, 0:512], ScSs[P][:, 0:128], QCD[P][:, t * 512:(t + 1) * 512],
                            start=True, stop=False), waits=wcc, inc=("mm", 1))
                        c2m = S.add("tensor", lambda t=t, P=P: nc.tensor.matmul(
                            ps_c[P][:, 0:512], ScSs[P][:, 128:256], QSD[P][:, t * 512:(t + 1) * 512],
                            start=False, stop=True), inc=("mm", 1))
                        for j in range(4):
                            cb = t * 4 + j
                            f = math.exp(-s * 128.0 * cb)
                            corr_ve = S.add("vector", lambda j=j, cb=cb, f=f, P=P:
                                            nc.vector.scalar_tensor_tensor(
                                                o_h[P][:, cb * 128:(cb + 1) * 128],
                                                ps_c[P][:, j * 128:(j + 1) * 128], f,
                                                o_h[P][:, cb * 128:(cb + 1) * 128], ALU.mult, ALU.add),
                                            waits=([("mm", c2m)] if j == 0 else []), inc=("ve", 1),
                                            sync=(j == 3))
                        yield
                    spill = S.add("sync", lambda h=h, P=P: nc.sync.dma_start(
                        out=oT_scr[h * 128:(h + 1) * 128, :], in_=o_h[P][:]),
                        waits=[("ve", corr_ve)], inc=("st", 16))
                    hstate[h]["spill"] = spill
                    hstate[h]["corr_ve"] = corr_ve
                    hstate[h]["mm_done"] = S.planned["mm"]

                active = []
                nxti = 0
                while active or nxti < H:
                    while len(active) < 2 and nxti < H:
                        active.append(head_pipeline(nxti))
                        nxti += 1
                    for g in list(active):
                        try:
                            next(g)
                        except StopIteration:
                            active.remove(g)

                ms["p2_st"] = max(hs["spill"] for hs in hstate)
                ms["p2_mm"] = max(hs["mm_done"] for hs in hstate)


        if "p3" in phases:
            p3 = contextlib.ExitStack()
            with p3:
                ps3 = [p3.enter_context(nc.psum_tensor(f"p3ps{i}", [128, 512], F32)) for i in range(8)]
                p3b = contextlib.ExitStack()
                p3b.__enter__()
                sb3 = lambda n, shp, dt: p3b.enter_context(nc.sbuf_tensor(n, shp, dt))
                oT = sb3("oT", [128, KK * NT], F32R)
                wpb = [sb3(f"wpb{i}", [128, KK * 128], F32R) for i in range(2)]
                xrow = [sb3(f"xrow{i}", [128, NT], F32) for i in range(2)]
                trow = [sb3(f"trow{i}", [128, NT], F32) for i in range(2)]

                S.fence()
                m_oT = None
                for kk in range(KK):
                    m_oT = S.add("sync", lambda kk=kk: nc.sync.dma_start(
                        out=oT[:, kk * NT:(kk + 1) * NT], in_=oT_scr[kk * 128:(kk + 1) * 128, :].bitcast(F32R)),
                        inc=("ld", 16))

                bank_free3 = [None] * 8
                blk_mm = [None] * 16
                ev_ve = [None] * 16
                sp_st = [None] * 16
                xld_ms = [None] * 16
                for ob in range(16):
                    w = [("ld", m_oT)] if ob == 0 else []
                    if ob >= 2:
                        w.append(("mm", blk_mm[ob - 2]))
                    wl = S.add("sync", lambda ob=ob: nc.sync.dma_start(out=wpb[ob % 2][:], in_=wproj[ob]),
                               waits=w, inc=("ld", 16))
                    w2 = [("ve", ev_ve[ob - 2])] if ob >= 2 else []
                    xld_ms[ob] = S.add("sync", lambda ob=ob: nc.sync.dma_start(
                        out=xrow[ob % 2][:], in_=xT_scr[ob * 128:(ob + 1) * 128, :]),
                        waits=w2, inc=("ld", 16))
                    mlast = None
                    for kk in range(KK):
                        for tt in range(TT):
                            b = (ob % 2) * 4 + tt
                            tw = []
                            if kk == 0:
                                tw = [("ld", wl)]
                                if bank_free3[b] is not None:
                                    tw.append(("ve", bank_free3[b]))
                            mlast = S.add("tensor", lambda b=b, kk=kk, tt=tt, ob=ob:
                                          nc.tensor.matmul(ps3[b][:, 0:512],
                                                           wpb[ob % 2][:, kk * 128:(kk + 1) * 128],
                                                           oT[:, kk * NT + tt * 512: kk * NT + (tt + 1) * 512],
                                                           start=(kk == 0), stop=(kk == KK - 1)),
                                          waits=tw,
                                          inc=("mm", 1) if (kk == KK - 1 and tt == TT - 1) else None)
                    blk_mm[ob] = S.planned["mm"]

                    def ev(ob=ob):
                        ins = None
                        for tt in range(TT):
                            b = (ob % 2) * 4 + tt
                            ins = nc.vector.tensor_add(trow[ob % 2][:, tt * 512:(tt + 1) * 512],
                                                       ps3[b][:, 0:512],
                                                       xrow[ob % 2][:, tt * 512:(tt + 1) * 512])
                        return ins
                    wv = [("mm", blk_mm[ob]), ("ld", xld_ms[ob])]
                    if ob >= 2:
                        wv.append(("st", sp_st[ob - 2]))
                    ev_ve[ob] = S.add("vector", ev, waits=wv, inc=("ve", 1))
                    for tt in range(TT):
                        bank_free3[(ob % 2) * 4 + tt] = ev_ve[ob]
                    sp_st[ob] = S.add("scalar", lambda ob=ob: nc.scalar.dma_start(
                        out=tmp_scr[ob * 128:(ob + 1) * 128, :], in_=trow[ob % 2][:]),
                        waits=[("ve", ev_ve[ob])], inc=("st", 16))
                m_p3b_st = sp_st[15]
                p3b.__exit__(None, None, None)
                p36 = contextlib.ExitStack()
                es.enter_context(p36)
                x1b = p36.enter_context(nc.sbuf_tensor("x1b", [128, KK * NT], BF16))
                acc = p36.enter_context(nc.sbuf_tensor("acc", [128, KK * NT], BF16))
                sb3c = lambda n, shp, dt: p3.enter_context(nc.sbuf_tensor(n, shp, dt))
                srowlog = es.enter_context(nc.sbuf_tensor("srowlog", [128, NCH * 8], F32)) if "srow" in dbg else None
                tcol = [sb3c(f"tcol{i}", [128, KK * 128], F32) for i in range(2)]
                ttok = sb3c("ttok", [128, D], F32)
                sqt = sb3c("sqt", [128, D], F32)
                xtok = sb3c("xtok", [128, D], F32)
                srow = sb3c("srow", [128, 32], F32)
                S.fence()

                tmp_v = tmp_scr.rearrange("(kk p) t -> kk p t", p=128)
                tg_ld = [None] * NCH
                tg_fwdmm = [None] * NCH
                tg_btmm = [None] * NCH
                tg_done_ve = [None] * NCH
                prev_bt_mm = None
                prev_ev2 = None
                prev_fwd_ve = None
                prev_apply_ve = None
                for tg in range(NCH):
                    buf = tcol[tg % 2]
                    w = []
                    if tg == 0:
                        w = [("st", m_p3b_st)]
                    if tg >= 2:
                        w.append(("mm", tg_fwdmm[tg - 2]))
                    lm = None
                    for kk in range(KK):
                        wk = list(w) if kk == 0 else []
                        lm = S.add("sync", lambda kk=kk, tg=tg, buf=buf: nc.sync.dma_start(
                            out=buf[:, kk * 128:(kk + 1) * 128], in_=tmp_v[kk, :, tg * 128:(tg + 1) * 128]),
                            waits=wk, inc=("ld", 16))
                    tg_ld[tg] = lm
                    fwd_ve = None
                    for g4 in range(4):
                        bank = ps3[g4 % 2]
                        wv2 = [("ld", lm)]
                        if prev_fwd_ve is not None and g4 < 2:
                            wv2.append(("ve", prev_fwd_ve))
                        if g4 >= 2 and fwd_ve is not None:
                            wv2.append(("ve", fwd_ve))
                        ml = None
                        for j in range(4):
                            kk = g4 * 4 + j
                            ml = S.add("tensor", lambda bank=bank, j=j, kk=kk, buf=buf: nc.tensor.transpose(
                                bank[:, j * 128:(j + 1) * 128], buf[:, kk * 128:(kk + 1) * 128], idf[:]),
                                       waits=(wv2 if j == 0 else []), inc=("mm", 1))

                        def fev(bank=bank, g4=g4):
                            ins = None
                            for j in range(4):
                                kk = g4 * 4 + j
                                ins = nc.vector.tensor_copy(ttok[:, kk * 128:(kk + 1) * 128],
                                                            bank[:, j * 128:(j + 1) * 128])
                            return ins
                        wf = [("mm", ml)]
                        if prev_apply_ve is not None:
                            wf.append(("ve", prev_apply_ve))
                        fwd_ve = S.add("vector", fev, waits=wf, inc=("ve", 1), sync=(g4 == 3))
                        tg_fwdmm[tg] = ml
                    prev_fwd_ve = fwd_ve
                    m_s1 = S.add("vector", lambda: nc.vector.reduce_sum(srow[:, 0:1], ttok[:], axis=AX.X),
                                 waits=[("ve", fwd_ve)], inc=("ve", 1), sync=True)
                    m_s2 = S.add("scalar", lambda: nc.scalar.mul(srow[:, 4:5], srow[:, 0:1], consts[:, 11:12]),
                                 waits=[("ve", m_s1)], inc=("ac", 1))
                    m_a1 = S.add("vector", lambda: nc.vector.tensor_scalar_sub(ttok[:], ttok[:], srow[:, 4:5]),
                                 waits=[("ac", m_s2)], inc=("ve", 1), sync=True)
                    m_sq = S.add("scalar", lambda: nc.scalar.activation(sqt[:], ttok[:], AF.Square),
                                 waits=[("ve", m_a1)], inc=("ac", 1))
                    m_s3 = S.add("vector", lambda: nc.vector.reduce_sum(srow[:, 8:9], sqt[:], axis=AX.X),
                                 waits=[("ac", m_sq)], inc=("ve", 1), sync=True)
                    m_s4 = S.add("scalar", lambda: nc.scalar.mul(srow[:, 16:17], srow[:, 8:9], consts[:, 11:12]),
                                 waits=[("ve", m_s3)], inc=("ac", 1))
                    m_s5 = S.add("scalar", lambda: nc.scalar.add(srow[:, 16:17], srow[:, 16:17], consts[:, 1:2]),
                                 inc=("ac", 1))
                    m_ln = S.add("scalar", lambda: nc.scalar.activation(srow[:, 20:21], srow[:, 16:17], AF.Ln),
                                 inc=("ac", 1))
                    m_lh = S.add("scalar", lambda: nc.scalar.mul(srow[:, 28:29], srow[:, 20:21], consts[:, 4:5]),
                                 inc=("ac", 1))
                    m_rs = S.add("scalar", lambda: nc.scalar.activation(srow[:, 24:25], srow[:, 28:29], AF.Exp),
                                 inc=("ac", 1))
                    wa = [("ac", m_rs)]
                    if prev_bt_mm is not None:
                        wa.append(("mm", prev_bt_mm))
                    m_a2 = S.add("vector", lambda: nc.vector.tensor_scalar_mul(xtok[:], ttok[:], srow[:, 24:25]),
                                 waits=wa, inc=("ve", 1), sync=True)
                    prev_apply_ve = m_a2
                    if srowlog is not None:
                        m_a2 = S.add("vector", lambda tg=tg: nc.vector.tensor_copy(
                            srowlog[:, tg * 8:(tg + 1) * 8],
                            srow[:].rearrange("p (a b) -> p a b", b=4)[:, :, 0:1].rearrange("p a b -> p (a b)")),
                            inc=("ve", 1))
                        prev_apply_ve = m_a2
                    ev2 = None
                    for g4 in range(4):
                        bank = ps3[2 + g4 % 2]
                        wv3 = [("ve", m_a2)]
                        if prev_ev2 is not None and g4 < 2:
                            wv3.append(("ve", prev_ev2))
                        if g4 >= 2 and ev2 is not None:
                            wv3.append(("ve", ev2))
                        ml = None
                        for j in range(4):
                            kk = g4 * 4 + j
                            ml = S.add("tensor", lambda bank=bank, j=j, kk=kk: nc.tensor.transpose(
                                bank[:, j * 128:(j + 1) * 128], xtok[:, kk * 128:(kk + 1) * 128], idf[:]),
                                       waits=(wv3 if j == 0 else []), inc=("mm", 1))

                        def bev(bank=bank, g4=g4, tg=tg):
                            ins = None
                            for j in range(4):
                                kk = g4 * 4 + j
                                dst = slice(kk * NT + tg * 128, kk * NT + (tg + 1) * 128)
                                nc.vector.tensor_copy(x1b[:, dst], bank[:, j * 128:(j + 1) * 128])
                                ins = nc.vector.tensor_scalar_mul(acc[:, dst], bank[:, j * 128:(j + 1) * 128], 2.0)
                            return ins
                        ev2 = S.add("vector", bev, waits=[("mm", ml)], inc=("ve", 1))
                        tg_btmm[tg] = ml
                    prev_bt_mm = tg_btmm[tg]
                    prev_ev2 = ev2
                    tg_done_ve[tg] = ev2
                ms["p3_ve"] = tg_done_ve[NCH - 1]


        if "p4" in phases:
            p4 = contextlib.ExitStack()
            comb = es.enter_context(nc.sbuf_tensor("comb", [128, NCH * 8], F32))
            p4log = es.enter_context(nc.sbuf_tensor("p4log", [128, 8 * 32], F32)) if "p4log" in dbg else None
            with p4:
                sb4 = lambda n, shp, dt: p4.enter_context(nc.sbuf_tensor(n, shp, dt))
                wrts = sb4("wrts", [128, KK * 8], F32)
                wrtb = sb4("wrtb", [128, KK * 8], BF16)
                lg = sb4("lg", [128, 8], F32)
                el = sb4("el", [128, 8], F32)
                wts = sb4("wts", [128, 8], F32)
                wre = sb4("wre", [128, 8], F32)
                msk = sb4("msk", [128, 8], F32)
                rcol = sb4("rcol", [128, 12], F32)
                ps4 = [p4.enter_context(nc.psum_tensor(f"p4ps{i}", [128, 512], F32)) for i in range(2)]
                S.fence()
                m_w = S.add("sync", lambda: nc.sync.dma_start(out=wrts[:], in_=wrt), inc=("ld", 16))
                S.add("vector", lambda: nc.vector.memset(rcol[:], 0.0), inc=("ve", 1))
                m_wb = S.add("vector", lambda: nc.vector.tensor_copy(wrtb[:], wrts[:]),
                             waits=[("ld", m_w)], inc=("ve", 1))
                prev_lg_ve = None
                for tg in range(NCH):
                    bank = ps4[tg % 2]
                    wt0 = [("ve", m_wb)]
                    if prev_lg_ve is not None and tg >= 2:
                        wt0.append(("ve", prev_lg_ve))
                    ml = None
                    for kk in range(KK):
                        ml = S.add("tensor", lambda bank=bank, kk=kk, tg=tg: nc.tensor.matmul(
                            bank[:, 0:8], x1b[:, kk * NT + tg * 128: kk * NT + (tg + 1) * 128],
                            wrtb[:, kk * 8:(kk + 1) * 8],
                            start=(kk == 0), stop=(kk == KK - 1)),
                            waits=(wt0 if kk == 0 else []),
                            inc=("mm", 1) if kk == KK - 1 else None)
                    mv = S.add("vector", lambda bank=bank: nc.vector.tensor_copy(lg[:], bank[:, 0:8]),
                               waits=[("mm", ml)], inc=("ve", 1), sync=True)
                    prev_lg_ve = mv
                    r1 = S.add("vector", lambda: nc.vector.reduce_max(rcol[:, 0:1], lg[:], axis=AX.X), inc=("ve", 1), sync=True)
                    bb = S.add("scalar", lambda: nc.scalar.mul(rcol[:, 1:2], rcol[:, 0:1], -1.0),
                               waits=[("ve", r1)], inc=("ac", 1))
                    ma = S.add("scalar", lambda: nc.scalar.activation(el[:], lg[:], AF.Exp, bias=rcol[:, 1:2]),
                               inc=("ac", 1))
                    r2 = S.add("vector", lambda: nc.vector.reduce_sum(rcol[:, 3:4], el[:], axis=AX.X),
                               waits=[("ac", ma)], inc=("ve", 1), sync=True)
                    bs = S.add("scalar", lambda: nc.scalar.copy(rcol[:, 4:5], rcol[:, 3:4]),
                               waits=[("ve", r2)], inc=("ac", 1))
                    mv = S.add("vector", lambda: nc.vector.reciprocal(rcol[:, 9:10], rcol[:, 4:5]),
                               waits=[("ac", bs)], inc=("ve", 1), sync=True)
                    br = S.add("scalar", lambda: nc.scalar.copy(rcol[:, 10:11], rcol[:, 9:10]),
                               waits=[("ve", mv)], inc=("ac", 1))
                    mv = S.add("vector", lambda: nc.vector.tensor_scalar_mul(wts[:], el[:], rcol[:, 10:11]),
                               waits=[("ac", br)], inc=("ve", 1), sync=True)
                    r3 = S.add("vector", lambda: nc.vector.reduce_max(rcol[:, 5:6], wts[:], axis=AX.X), inc=("ve", 1), sync=True)
                    bm = S.add("scalar", lambda: nc.scalar.copy(rcol[:, 6:7], rcol[:, 5:6]),
                               waits=[("ve", r3)], inc=("ac", 1))
                    mv = S.add("vector", lambda: nc.vector.tensor_single_scalar(msk[:], wts[:], rcol[:, 6:7],
                                                                                ALU.is_equal),
                               waits=[("ac", bm)], inc=("ve", 1), sync=True)
                    mv = S.add("vector", lambda: nc.vector.scalar_tensor_tensor(wre[:], msk[:], -1e30, wts[:],
                                                                                ALU.mult, ALU.add), inc=("ve", 1), sync=True)
                    r4 = S.add("vector", lambda: nc.vector.reduce_max(rcol[:, 7:8], wre[:], axis=AX.X), inc=("ve", 1), sync=True)
                    bm2 = S.add("scalar", lambda: nc.scalar.copy(rcol[:, 8:9], rcol[:, 7:8]),
                                waits=[("ve", r4)], inc=("ac", 1))
                    mv = S.add("vector", lambda: nc.vector.tensor_single_scalar(msk[:], wts[:], rcol[:, 8:9],
                                                                                ALU.is_ge),
                               waits=[("ac", bm2)], inc=("ve", 1), sync=True)
                    mv = S.add("vector", lambda tg=tg: nc.vector.tensor_mul(comb[:, tg * 8:(tg + 1) * 8],
                                                                            wts[:], msk[:]), inc=("ve", 1), sync=True)
                    if p4log is not None and tg < 8:
                        S.add("vector", lambda tg=tg: nc.vector.tensor_copy(
                            p4log[:, tg * 32:tg * 32 + 8], lg[:]), inc=("ve", 1))
                        S.add("vector", lambda tg=tg: nc.vector.tensor_copy(
                            p4log[:, tg * 32 + 8:tg * 32 + 16], el[:]), inc=("ve", 1))
                        S.add("vector", lambda tg=tg: nc.vector.tensor_copy(
                            p4log[:, tg * 32 + 16:tg * 32 + 24], wts[:]), inc=("ve", 1))
                        mv = S.add("vector", lambda tg=tg: nc.vector.tensor_copy(
                            p4log[:, tg * 32 + 24:tg * 32 + 32], rcol[:, 0:12:2].rearrange("p a -> p a") if False else rcol[:, 0:8]), inc=("ve", 1))
                ms["p4_ve"] = mv

        if "p5" in phases:
            p5 = contextlib.ExitStack()
            with p5:
                sb5 = lambda n, shp, dt: p5.enter_context(nc.sbuf_tensor(n, shp, dt))
                hbuf = sb5("hbuf", [128, NII * NT], BF16)
                gu_f0 = sb5("guf0", [128, NT // 2], F32)
                gu_b = [sb5(f"gub{i}", [128, NT], BF16) for i in range(2)]
                dn_f = sb5("dnf", [128, FF], F32)
                dn_b = [sb5(f"dnb{i}", [128, FF], BF16) for i in range(2)]
                combB = sb5("combB", [128, NT], BF16)
                crow = sb5("crow", [1, NT // 4], F32R)
                sigb = sb5("sigb", [128, 512], F32)
                ps5 = [p5.enter_context(nc.psum_tensor(f"p5ps{i}", [128, 512], F32)) for i in range(8)]
                S.fence()

                path_down_mm = None
                prev_gu_mm = {}
                prev_acc_ve = None
                prev_comb_ve = None
                for p_i in range(NPATH):
                    if p_i >= 1:
                        e = p_i - 1
                        bve = None
                        for hh in range(4):
                            cve = None
                            for tgl in range(4):
                                tg = hh * 4 + tgl
                                bank = ps5[6 + (tg % 2)]
                                wtc = [("ve", ms["p4_ve"])]
                                if prev_acc_ve is not None:
                                    wtc.append(("ve", prev_acc_ve))
                                if prev_comb_ve is not None:
                                    wtc.append(("ve", prev_comb_ve))
                                if bve is not None and tgl == 0:
                                    wtc.append(("ve", bve))
                                if cve is not None and tgl >= 2:
                                    wtc.append(("ve", cve))
                                mt = S.add("tensor", lambda bank=bank, tg=tg, e=e: nc.tensor.transpose(
                                    bank[0:1, 0:128], comb[:, tg * 8 + e: tg * 8 + e + 1], idf[:]),
                                    waits=wtc, inc=("mm", 1))
                                cve = S.add("vector", lambda bank=bank, tgl=tgl: nc.vector.tensor_copy(
                                    crow[0:1, tgl * 128:(tgl + 1) * 128], bank[0:1, 0:128]),
                                    waits=[("mm", mt)], inc=("ve", 1), sync=True)
                            bank = ps5[6 + (hh % 2)]
                            mt = S.add("tensor", lambda bank=bank: nc.tensor.matmul(
                                bank[:, 0:512], ones1[:], crow[0:1, 0:512],
                                start=True, stop=True), waits=[("ve", cve)], inc=("mm", 1))
                            bve = S.add("vector", lambda bank=bank, hh=hh: nc.vector.tensor_copy(
                                combB[:, hh * 512:(hh + 1) * 512], bank[:, 0:512]),
                                waits=[("mm", mt)], inc=("ve", 1), sync=True)
                        prev_comb_ve = bve
                        m_cB = bve
                    else:
                        m_cB = None

                    unit_ev = {}
                    qcast = {}
                    hav = {}
                    last_unit_mm = {}
                    qfree = {}
                    for ii in range(NII):
                        prev_cast = None
                        for q in range(4):
                            half, qq = q // 2, q % 2
                            wld = []
                            if prev_cast is not None:
                                wld.append(("ve", prev_cast))
                            if ii >= 1 and (ii - 1, q) in qfree:
                                wld.append(("mm", qfree[(ii - 1, q)]))
                            elif ii == 0:
                                if path_down_mm is not None:
                                    wld.append(("mm", path_down_mm))
                                if prev_acc_ve is not None:
                                    wld.append(("ve", prev_acc_ve))
                            lm = S.add("sync", lambda p_i=p_i, ii=ii, half=half, qq=qq: nc.sync.dma_start(
                                out=gu_f0[:], in_=wgu[p_i, ii, half][:, qq * 1024:(qq + 1) * 1024]),
                                waits=wld, inc=("ld", 16))
                            prev_cast = S.add("vector", lambda half=half, qq=qq: nc.vector.tensor_copy(
                                gu_b[half][:, qq * 1024:(qq + 1) * 1024], gu_f0[:]),
                                waits=[("ld", lm)], inc=("ve", 1), sync=True)
                            qcast[(ii, q)] = prev_cast
                        for u2 in range(2):
                            bs = 0 if (ii * 2 + u2) % 2 == 0 else 4
                            banks = [ps5[bs], ps5[bs + 1], ps5[bs + 2], ps5[bs + 3]]
                            mlast = None
                            for kk in range(KK):
                                half = kk // 8
                                kk8 = kk % 8
                                tw = []
                                if kk % 4 == 0:
                                    tw = [("ve", qcast[(ii, kk // 4)])]
                                    if kk == 0:
                                        uk = (ii * 2 + u2 - 2)
                                        if uk >= 0:
                                            pii, pu2 = uk // 2, uk % 2
                                            if (pii, pu2) in unit_ev:
                                                tw.append(("ve", unit_ev[(pii, pu2)]))
                                for gi in range(2):
                                    for ti in range(2):
                                        tt = u2 * 2 + ti
                                        b = banks[gi * 2 + ti]
                                        want_inc = (gi == 1 and ti == 1 and kk % 4 == 3)
                                        mv = S.add("tensor",
                                                   lambda b=b, kk8=kk8, half=half, gi=gi, tt=tt, kk=kk:
                                                   nc.tensor.matmul(
                                                       b[:, 0:512],
                                                       gu_b[half][:, kk8 * 256 + gi * 128: kk8 * 256 + gi * 128 + 128],
                                                       x1b[:, kk * NT + tt * 512: kk * NT + (tt + 1) * 512],
                                                       start=(kk == 0), stop=(kk == KK - 1)),
                                                   waits=(tw if (gi == 0 and ti == 0) else []),
                                                   inc=("mm", 1) if want_inc else None)
                                        if want_inc and u2 == 1:
                                            qfree[(ii, kk // 4)] = S.planned["mm"]
                            last_unit_mm[(ii, u2)] = S.planned["mm"]
                            wsg = [("mm", last_unit_mm[(ii, u2)])]
                            uk = (ii * 2 + u2 - 1)
                            if uk >= 0:
                                pii, pu2 = uk // 2, uk % 2
                                if (pii, pu2) in unit_ev:
                                    wsg.append(("ve", unit_ev[(pii, pu2)]))

                            uev_m = None
                            for ti in range(2):
                                tt = u2 * 2 + ti
                                dsl = slice(ii * NT + tt * 512, ii * NT + (tt + 1) * 512)
                                wsg_t = list(wsg) if ti == 0 else []
                                if ti == 1 or uev_m is not None:
                                    wsg_t.append(("ve", uev_m if uev_m is not None else 0))
                                wsg_t = [(a2, b2) for a2, b2 in wsg_t if b2]
                                a_sig = S.add("scalar", lambda banks=banks, ti=ti: nc.scalar.activation(
                                    sigb[:, 0:512], banks[ti][:, 0:512], AF.Sigmoid),
                                    waits=wsg_t, inc=("ac", 1))
                                wu = [("ac", a_sig)]
                                if ti == 0:
                                    if path_down_mm is not None:
                                        wu.append(("mm", path_down_mm))
                                    if m_cB is not None:
                                        wu.append(("ve", m_cB))
                                if p_i >= 1:
                                    S.add("vector", lambda dsl=dsl, banks=banks, ti=ti: nc.vector.tensor_mul(
                                        hbuf[:, dsl], sigb[:, 0:512], banks[2 + ti][:, 0:512]),
                                        waits=wu, inc=("ve", 1))
                                    uev_m = S.add("vector", lambda dsl=dsl, tt=tt: nc.vector.tensor_mul(
                                        hbuf[:, dsl], hbuf[:, dsl], combB[:, tt * 512:(tt + 1) * 512]),
                                        inc=("ve", 1), sync=True)
                                else:
                                    uev_m = S.add("vector", lambda dsl=dsl, banks=banks, ti=ti: nc.vector.tensor_mul(
                                        hbuf[:, dsl], sigb[:, 0:512], banks[2 + ti][:, 0:512]),
                                        waits=wu, inc=("ve", 1), sync=True)
                            unit_ev[(ii, u2)] = uev_m
                            hav[(ii, u2)] = uev_m

                    dn_last_mm = None
                    dn_ev = {}
                    dcast = {}
                    for of in range(16):
                        wld = []
                        if of >= 1:
                            wld.append(("ve", dcast[of - 1]))
                        if of >= 2 and (of - 2) in dn_ev:
                            wld.append(("ve", dn_ev[of - 2]))
                        lm = S.add("sync", lambda p_i=p_i, of=of: nc.sync.dma_start(
                            out=dn_f[:], in_=wdn[p_i, of]), waits=wld, inc=("ld", 16))
                        dcast[of] = S.add("vector", lambda of=of: nc.vector.tensor_copy(
                            dn_b[of % 2][:], dn_f[:]), waits=[("ld", lm)], inc=("ve", 1), sync=True)
                        mlast = None
                        for ii in range(NII):
                            for tt in range(TT):
                                b = ps5[(of % 2) * 4 + tt]
                                tw = []
                                if ii == 0 and tt == 0:
                                    tw = [("ve", dcast[of]), ("ve", hav[(NII - 1, 1)])]
                                    if of >= 2 and (of - 2) in dn_ev:
                                        tw.append(("ve", dn_ev[of - 2]))
                                mlast = S.add("tensor", lambda b=b, of=of, ii=ii, tt=tt: nc.tensor.matmul(
                                    b[:, 0:512], dn_b[of % 2][:, ii * 128:(ii + 1) * 128],
                                    hbuf[:, ii * NT + tt * 512: ii * NT + (tt + 1) * 512],
                                    start=(ii == 0), stop=(ii == NII - 1)),
                                    waits=tw, inc=("mm", 1) if (ii == NII - 1 and tt == TT - 1) else None)
                        dn_last_mm = S.planned["mm"]

                        def dev(of=of):
                            ins = None
                            for tt in range(TT):
                                b = ps5[(of % 2) * 4 + tt]
                                dsl = slice(of * NT + tt * 512, of * NT + (tt + 1) * 512)
                                ins = nc.vector.tensor_add(acc[:, dsl], acc[:, dsl], b[:, 0:512])
                            return ins
                        wd = [("mm", dn_last_mm)]
                        dn_ev[of] = S.add("vector", dev, waits=wd, inc=("ve", 1))
                    path_down_mm = dn_last_mm
                    prev_acc_ve = dn_ev[15]
                ms["p5_ve"] = prev_acc_ve
                ms["p5_mm"] = path_down_mm

        if "p6" in phases:
            p6 = contextlib.ExitStack()
            with p6:
                sb6 = lambda n, shp, dt: p6.enter_context(nc.sbuf_tensor(n, shp, dt))
                atok = sb6("atok", [128, D], F32)
                sq6 = sb6("sq6", [128, D], F32)
                otok = [sb6(f"otok{i}", [128, D], F32) for i in range(2)]
                sr6 = sb6("sr6", [128, 32], F32)
                pb6 = [p6.enter_context(nc.psum_tensor(f"p6ps{i}", [128, 1024], BF16)) for i in range(2)]
                S.fence()
                prev_apply6 = None
                out_st = None
                prev_fwd6 = None
                for tg in range(NCH):
                    fwd_ve = None
                    for g8 in range(2):
                        bank = pb6[g8]
                        wv = []
                        if prev_fwd6 is not None:
                            wv.append(("ve", prev_fwd6))
                        ml = None
                        for j in range(8):
                            kk = g8 * 8 + j
                            ml = S.add("tensor", lambda bank=bank, j=j, kk=kk, tg=tg: nc.tensor.transpose(
                                bank[:, j * 128:(j + 1) * 128],
                                acc[:, kk * NT + tg * 128: kk * NT + (tg + 1) * 128], idb[:]),
                                waits=(wv if j == 0 else []), inc=("mm", 1))

                        def fev6(bank=bank, g8=g8):
                            ins = None
                            for j in range(8):
                                kk = g8 * 8 + j
                                ins = nc.vector.tensor_copy(atok[:, kk * 128:(kk + 1) * 128],
                                                            bank[:, j * 128:(j + 1) * 128])
                            return ins
                        wf = [("mm", ml)]
                        if prev_apply6 is not None:
                            wf.append(("ve", prev_apply6))
                        fwd_ve = S.add("vector", fev6, waits=wf, inc=("ve", 1), sync=(g8 == 1))
                    prev_fwd6 = fwd_ve
                    m_s1 = S.add("vector", lambda: nc.vector.reduce_sum(sr6[:, 0:1], atok[:], axis=AX.X),
                                 waits=[("ve", fwd_ve)], inc=("ve", 1), sync=True)
                    m_s2 = S.add("scalar", lambda: nc.scalar.mul(sr6[:, 4:5], sr6[:, 0:1], consts[:, 11:12]),
                                 waits=[("ve", m_s1)], inc=("ac", 1))
                    m_a1 = S.add("vector", lambda: nc.vector.tensor_scalar_sub(atok[:], atok[:], sr6[:, 4:5]),
                                 waits=[("ac", m_s2)], inc=("ve", 1), sync=True)
                    m_sq = S.add("scalar", lambda: nc.scalar.activation(sq6[:], atok[:], AF.Square),
                                 waits=[("ve", m_a1)], inc=("ac", 1))
                    m_s3 = S.add("vector", lambda: nc.vector.reduce_sum(sr6[:, 8:9], sq6[:], axis=AX.X),
                                 waits=[("ac", m_sq)], inc=("ve", 1), sync=True)
                    m_s4 = S.add("scalar", lambda: nc.scalar.mul(sr6[:, 16:17], sr6[:, 8:9], consts[:, 11:12]),
                                 waits=[("ve", m_s3)], inc=("ac", 1))
                    m_s5 = S.add("scalar", lambda: nc.scalar.add(sr6[:, 16:17], sr6[:, 16:17], consts[:, 1:2]),
                                 inc=("ac", 1))
                    m_ln = S.add("scalar", lambda: nc.scalar.activation(sr6[:, 20:21], sr6[:, 16:17], AF.Ln),
                                 inc=("ac", 1))
                    m_lh = S.add("scalar", lambda: nc.scalar.mul(sr6[:, 28:29], sr6[:, 20:21], consts[:, 4:5]),
                                 inc=("ac", 1))
                    m_rs = S.add("scalar", lambda: nc.scalar.activation(sr6[:, 24:25], sr6[:, 28:29], AF.Exp),
                                 inc=("ac", 1))
                    ob = otok[tg % 2]
                    wa = [("ac", m_rs)]
                    if out_st is not None and tg >= 2:
                        wa.append(("st", out_st))
                    m_a2 = S.add("vector", lambda ob=ob: nc.vector.tensor_scalar_mul(ob[:], atok[:], sr6[:, 24:25]),
                                 waits=wa, inc=("ve", 1), sync=True)
                    prev_apply6 = m_a2
                    out_st = S.add("scalar", lambda ob=ob, tg=tg: nc.scalar.dma_start(
                        out=out[tg * 128:(tg + 1) * 128, :], in_=ob[:]),
                        waits=[("ve", m_a2)], inc=("st", 16))
                ms["p6_st"] = out_st

        if dbg:
            wait_all = [("st", S.planned.get("st", 0)), ("mm", S.planned.get("mm", 0)),
                        ("ve", S.planned.get("ve", 0)), ("ac", S.planned.get("ac", 0)),
                        ("gp", S.planned.get("gp", 0)), ("ld", S.planned.get("ld", 0))]
            srcs = {"qkvT": qkvT, "kvT": kvT, "oT": oT_scr, "tmp": tmp_scr}
            if "p3" in phases:
                srcs["x1b"] = x1b[:]
                srcs["acc"] = acc[:]
            if "p3" in phases and srowlog is not None:
                srcs["srow"] = srowlog[:]
            if "p4" in phases:
                srcs["comb"] = comb[:]
                if p4log is not None:
                    srcs["p4log"] = p4log[:]
            first = True
            for name, dst in dbg_outs.items():
                if name in srcs:
                    S.add("gpsimd", lambda dst=dst, name=name: nc.gpsimd.dma_start(out=dst, in_=srcs[name]),
                          waits=(wait_all if first else []), inc=("gd", 16))
                    first = False
            S.add("gpsimd", lambda: nc.gpsimd.dma_start(out=out[0:1, 0:1], in_=xo[0:1, 0:1]), inc=("gd", 16))
            S.add("gpsimd", lambda: nc.gpsimd.engine_nop(), waits=[("gd", S.planned["gd"])])
        elif "p6" in phases:
            S.add("gpsimd", lambda: nc.gpsimd.engine_nop(),
                  waits=[("st", S.planned.get("st", 0))])
        else:
            S.add("gpsimd", lambda: nc.gpsimd.dma_start(out=out, in_=xo),
                  waits=[("st", S.planned.get("st", 0)), ("mm", S.planned.get("mm", 0))], inc=("gd", 16))
            S.add("gpsimd", lambda: nc.gpsimd.engine_nop(), waits=[("gd", S.planned["gd"])])

        with nc.Block() as block:
            S.emit_all(block, sems)

    return nc


def layout_inputs(inp):
    x = np.asarray(inp["L0_attn_x"], np.float32)
    w_qkv = np.asarray(inp["L0_attn_w_qkv"], np.float32)
    w_proj = np.asarray(inp["L0_attn_w_proj"], np.float32)
    th = np.asarray(inp["L0_attn_theta"], np.float32).reshape(H, E)
    rw = np.asarray(inp["L0_ffn_router_w"], np.float32)
    sg = np.asarray(inp["L0_ffn_shared_gate"], np.float32)
    su = np.asarray(inp["L0_ffn_shared_up"], np.float32)
    sd = np.asarray(inp["L0_ffn_shared_down"], np.float32)
    wg = np.asarray(inp["L0_ffn_w_gate"], np.float32)
    wu = np.asarray(inp["L0_ffn_w_up"], np.float32)
    wd = np.asarray(inp["L0_ffn_w_down"], np.float32)

    wqkv_t = np.ascontiguousarray(
        w_qkv.reshape(48, 128, KK, 128).transpose(0, 3, 2, 1).reshape(48, 128, KK * 128))
    wproj_t = np.ascontiguousarray(
        w_proj.reshape(16, 128, KK, 128).transpose(0, 3, 2, 1).reshape(16, 128, KK * 128))

    def gu_tiles(g, u):
        gt = g.reshape(NII, 128, KK, 128).transpose(0, 3, 2, 1)
        ut = u.reshape(NII, 128, KK, 128).transpose(0, 3, 2, 1)
        s = np.stack([gt, ut], axis=3)
        s = s.reshape(NII, 128, 2, 8, 2, 128).transpose(0, 2, 1, 3, 4, 5)
        return s.reshape(NII, 2, 128, 8 * 2 * 128)
    gu_list = [gu_tiles(sg, su)] + [gu_tiles(wg[e], wu[e]) for e in range(NE)]
    wgu_t = np.ascontiguousarray(np.stack(gu_list, axis=0))

    def dn_tiles(w):
        t = w.reshape(16, 128, NII, 128).transpose(0, 3, 2, 1)
        return t.reshape(16, 128, NII * 128)
    dn_list = [dn_tiles(sd)] + [dn_tiles(wd[e]) for e in range(NE)]
    wdn_t = np.ascontiguousarray(np.stack(dn_list, axis=0))

    wrt_t = np.ascontiguousarray(rw.T.reshape(KK, 128, 8).transpose(1, 0, 2).reshape(128, KK * 8))

    in_maps = []
    for c in range(8):
        b, p = c // 2, c % 2
        x_own = np.ascontiguousarray(x[b, p * 2048:(p + 1) * 2048, :])
        x_oth = np.ascontiguousarray(x[b, (1 - p) * 2048:(2 - p) * 2048, :])
        t0own = np.full((128, 1), float(p * 2048), np.float32)
        t0oth = np.full((128, 1), float((1 - p) * 2048), np.float32)
        mu = np.full((128, 1), float(p), np.float32)
        in_maps.append({
            "xo": x_own, "xh": x_oth, "wqkv": wqkv_t, "wproj": wproj_t,
            "wgu": wgu_t, "wdn": wdn_t, "wrt": wrt_t, "theta": th,
            "t0o": t0own, "t0h": t0oth, "muse": mu,
        })
    return in_maps


def kernel(**inputs):
    in_maps = layout_inputs(inputs)
    nc = build()
    res = run_bass_kernel_spmd(nc, in_maps, core_ids=list(range(8)))
    outs = [np.asarray(res.results[c]["out"]) for c in range(8)]
    full = np.stack(outs, axis=0).reshape(4, 2, 2048, 2048).reshape(4, 4096, 2048)
    return full
